# version 08 (final)
# baseline (speedup 1.0000x reference)
import numpy as np
import ml_dtypes
from contextlib import ExitStack

import concourse.bass as bass
import concourse.mybir as mybir

F32 = mybir.dt.float32
BF16 = mybir.dt.bfloat16
I32 = mybir.dt.int32
U32 = mybir.dt.uint32
A = mybir.AluOpType
AF = mybir.ActivationFunctionType
AX = mybir.AxisListType
ds = bass.ds
IOA = bass.IndirectOffsetOnAxis
bf = ml_dtypes.bfloat16

NCORES, TOK, NB, D, E, CAP, DFF, TT = 8, 2048, 4096, 1024, 8, 2560, 4096, 512
NG = TOK // 128
EPS = 1e-5
RG = [list(range(NCORES))]


class Ctr:
    def __init__(self, sem):
        self.sem, self.val = sem, 0

    def inc(self, binst, n=1):
        self.val += n
        binst.then_inc(self.sem, n)
        return binst


def build_nc(stage="full"):
    nc = bass.Bass("TRN2", target_bir_lowering=False)

    xb = nc.dram_tensor("xb", [NB, D], F32, kind="ExternalInput").ap()
    wk_in = nc.dram_tensor("wk", [D, 512], F32, kind="ExternalInput").ap()
    wv_in = nc.dram_tensor("wv", [D, 512], F32, kind="ExternalInput").ap()
    wq_in = nc.dram_tensor("wq", [D, D], F32, kind="ExternalInput").ap()
    wo_in = nc.dram_tensor("wo", [D, D], F32, kind="ExternalInput").ap()
    bout_in = nc.dram_tensor("bout2", [128, 8], F32, kind="ExternalInput").ap()
    wr_in = nc.dram_tensor("wr", [D, E], F32, kind="ExternalInput").ap()
    w1_in = nc.dram_tensor("w1e", [D, DFF], F32, kind="ExternalInput").ap()
    w2_in = nc.dram_tensor("w2e", [DFF, D], F32, kind="ExternalInput").ap()
    tokhi_in = nc.dram_tensor("tokhi", [128, NG], F32, kind="ExternalInput").ap()
    toklo_in = nc.dram_tensor("toklo", [128, NG], F32, kind="ExternalInput").ap()
    pref_in = nc.dram_tensor("prefmask", [E, 1], F32, kind="ExternalInput").ap()
    dyn_in = nc.dram_tensor("dyn", [1, 4], I32, kind="ExternalInput").ap()
    out_ext = nc.dram_tensor("out", [TOK, D], F32, kind="ExternalOutput").ap()
    ydbg = nc.dram_tensor("ydbg", [TOK, D], F32, kind="ExternalOutput").ap() if stage == "y" else None

    IT = lambda a, n: nc.inline_tensor(a, name=n).ap()
    ident8 = IT(np.eye(8, dtype=np.float32), "ident8")
    trilLc = IT((np.arange(128)[:, None] < np.arange(128)[None, :]).astype(bf), "trilLc")
    tril16c = IT((np.arange(16)[:, None] < np.arange(16)[None, :]).astype(np.float32), "tril16c")
    ones128c = IT(np.ones((128, 1), dtype=bf), "ones128c")
    ones16c = IT(np.ones((16, 1), dtype=np.float32), "ones16c")
    epsc = IT(np.full((128, 1), EPS, dtype=np.float32), "epsc")
    id1c = IT(np.eye(1, dtype=np.float32), "id1c")
    id128c = IT(np.eye(128, dtype=np.float32), "id128c")
    id128bc = IT(np.eye(128).astype(bf), "id128bc")
    _g = np.arange(128) // 8
    _j = np.arange(128) % 8
    LI128c = IT(((_g[:, None] < _g[None, :]) & (_j[:, None] == _j[None, :])).astype(np.float32), "LI128c")
    OJ8c = IT((np.arange(8)[:, None] == _j[None, :]).astype(np.float32), "OJ8c")
    ones1c = IT(np.ones((1, 128), dtype=np.float32), "ones1c")
    onesn1c = IT(np.full((1, 128), -1.0, dtype=np.float32), "onesn1c")
    iota8c = IT(np.tile(np.arange(8, dtype=np.float32), (128, 1)), "iota8c")
    iota128c = IT(np.tile(np.arange(128, dtype=np.float32), (128, 1)), "iota128c")
    iota160c = IT(np.tile(np.arange(160, dtype=np.float32), (128, 1)), "iota160c")

    DT = nc.dram_tensor
    xbf = DT("xbf", [NB, D], BF16)
    xbfo = DT("xbfo", [TOK, D], BF16)
    eT = DT("eT", [512, NB], BF16)
    aoT = DT("aoT", [D, TOK], BF16)
    ag1_in = DT("ag1_in", [512, 64], BF16)
    ag1_out = DT("ag1_out", [NCORES * 512, 64], BF16, addr_space="Shared")
    yag_in = DT("yag_in", [2064, D], BF16)
    yag_out = DT("yag_out", [NCORES * 2064, D], BF16, addr_space="Shared")
    zscr = DT("zscr", [TOK, D], BF16)
    z2scr = DT("z2scr", [TOK, D], BF16)
    cnt_in = DT("cnt_in", [1, E], F32)
    cnt_out = DT("cnt_out", [NCORES, E], F32, addr_space="Shared")
    crow_h = DT("crow_h", [1, 128], F32)
    s2_in = DT("s2_in", [128, 160], F32)
    s2_out = DT("s2_out", [128, 160], F32, addr_space="Shared")
    dispT = DT("dispT", [CAP, D], BF16)
    eoT = DT("eoT", [D, CAP], BF16)
    ag3_in = DT("ag3_in", [CAP, D], BF16)
    ag3_out = DT("ag3_out", [NCORES * CAP, D], BF16, addr_space="Shared")

    ctx = ExitStack()
    SBUF = lambda n, s, d, **kw: ctx.enter_context(nc.sbuf_tensor(n, s, d, **kw))
    SEM = lambda n: Ctr(ctx.enter_context(nc.semaphore(n)))
    bank = [ctx.enter_context(nc.psum_tensor(f"bank{i}", [128, 512], F32)) for i in range(8)]
    bank_free = [None] * 8

    def bw(t, i):
        if bank_free[i] is not None:
            c, v = bank_free[i]
            t.wait_ge(c.sem, v)

    sd = SEM("sd"); sd2 = SEM("sd2"); sd3 = SEM("sd3")
    cd = SEM("cd"); cd2 = SEM("cd2"); cd3 = SEM("cd3")
    gd = SEM("gd"); g2 = SEM("g2"); g3 = SEM("g3")
    mm = SEM("mm"); dv = SEM("dv"); ac = SEM("ac"); gp = SEM("gp")
    cc1, cc2, cc3, cc4, cc5 = [SEM(f"cc{i}") for i in range(1, 6)]
    gx = [SEM(f"gx{i}") for i in range(4)]
    g_wr = SEM("g_wr"); g_xo = SEM("g_xo"); g_wkv = SEM("g_wkv")
    g_wqo = SEM("g_wqo"); g_w1 = SEM("g_w1"); g_w2 = SEM("g_w2")
    g_ctxa = SEM("g_ctxa"); g_call = SEM("g_call"); g_s2l = SEM("g_s2l")
    g_crow = SEM("g_crow"); g_z2 = SEM("g_z2")
    cdo = [SEM("cdo0"), SEM("cdo1")]
    gq = [SEM(f"gq{i}") for i in range(4)]
    cq = [SEM(f"cq{i}") for i in range(4)]
    gy = [SEM("gy0"), SEM("gy1")]
    sa = [SEM("sa0"), SEM("sa1")]
    sme = [SEM("sme0"), SEM("sme1")]
    smg = [SEM("smg0"), SEM("smg1")]
    gz = [SEM("gz0"), SEM("gz1")]
    gsem = [g2, g3]
    xsem = [sd2, sd3]
    csem = [cd2, cd3]

    m_kv = SBUF("m_kv", [128, 4], F32); nm_kv = SBUF("nm_kv", [128, 4], F32)
    z_kv = SBUF("z_kv", [128, 4], F32); rz_kv = SBUF("rz_kv", [128, 4], F32)
    ctxd = SBUF("ctxd", [128, 8, 128], BF16)
    ctxa = SBUF("ctxa", [128, 8, 64], BF16)
    bout_sb = SBUF("bout_sb", [128, 8], F32)
    wr_sb = SBUF("wr_sb", [128, 8, 8], BF16)
    csn = SBUF("csn", [128, 8], F32); csr = SBUF("csr", [1, 8], F32)
    iota8 = SBUF("iota8", [128, 8], F32)
    iota128 = SBUF("iota128", [128, 128], F32)
    iota160 = SBUF("iota160", [128, 160], F32)
    trilL = SBUF("trilL", [128, 128], BF16)
    tril16 = SBUF("tril16", [16, 16], F32)
    ones128 = SBUF("ones128", [128, 1], BF16)
    ones16 = SBUF("ones16", [16, 1], F32)
    epss = SBUF("epss", [128, 1], F32)
    ones1 = SBUF("ones1", [1, 128], F32)
    onesn1 = SBUF("onesn1", [1, 128], F32)
    id8 = SBUF("id8", [8, 8], F32)
    id1 = SBUF("id1", [1, 1], F32)
    id128 = SBUF("id128", [128, 128], F32)
    id128b = SBUF("id128b", [128, 128], BF16)
    LI128 = SBUF("LI128", [128, 128], F32)
    OJ8 = SBUF("OJ8", [8, 128], F32)
    ccol = SBUF("ccol", [128, 1], F32)
    ocol = SBUF("ocol", [8, 1], F32)
    carry_col = SBUF("carry_col", [128, 1], F32)
    carry_row = SBUF("carry_row", [1, 128], F32)
    posc = SBUF("posc", [128, NG, 8], F32)
    tokhi = SBUF("tokhi_s", [128, NG], F32); toklo = SBUF("toklo_s", [128, NG], F32)
    pref = SBUF("pref_s", [E, 1], F32)
    dyn = SBUF("dyn_s", [1, 4], I32)
    pmax = SBUF("pmax", [128, NG], F32); etok = SBUF("etok", [128, NG], F32)
    post = SBUF("post", [128, NG], F32); kept = SBUF("kept", [128, NG], F32)
    kf = SBUF("kf", [128, NG], F32); k2 = SBUF("k2", [128, NG], F32)
    growi = SBUF("growi", [128, NG], I32)
    zsum = SBUF("zsum", [128, NG], F32); zsq = SBUF("zsq", [128, NG], F32)
    meanv = SBUF("meanv", [128, NG], F32); rstd = SBUF("rstd", [128, NG], F32)
    lg = SBUF("lg", [128, NG, 8], F32)
    lga = SBUF("lga", [128, NG, 8], F32)
    lgs = SBUF("lgs", [128, NG, 8], F32)
    lmax = SBUF("lmax", [128, NG], F32); se = SBUF("se", [128, NG], F32)
    tmpg = SBUF("tmpg", [128, NG], F32); tmpg2 = SBUF("tmpg2", [128, NG], F32)
    ohbf_all = SBUF("ohbf_all", [128, NG, 8], BF16)
    aidx = SBUF("aidx", [128, 8], U32)
    cnts = SBUF("cnts", [16, 8], F32); crow = SBUF("crow", [1, 128], F32)
    cntrow = SBUF("cntrow", [1, 8], F32)
    call = SBUF("call", [NCORES, 8], F32)
    offs = SBUF("offs", [1, 8], F32); goff = SBUF("goff", [16, 8], F32)
    u_T = SBUF("u_T", [8, 512], F32)
    zsc = SBUF("zsc", [128, D], BF16)
    s2sb = SBUF("s2sb", [128, 160], F32)
    s2i = SBUF("s2i", [128, 160], I32)
    s2my = SBUF("s2my", [128, 20], I32)
    iscr = SBUF("iscr", [128, 5 * NG], I32)
    rkps = SBUF("rkps", [128, 8], F32)

    block = ctx.enter_context(nc.Block())
    SY, T, V, SC, G = block.sync, block.tensor, block.vector, block.scalar, block.gpsimd

    def f(s):
        for (dst, src) in [(iota8, iota8c), (iota128, iota128c), (iota160, iota160c),
                           (trilL, trilLc), (tril16, tril16c), (ones128, ones128c),
                           (ones16, ones16c), (ones1, ones1c), (onesn1, onesn1c), (epss, epsc),
                           (id8, ident8), (id1, id1c), (id128, id128c), (id128b, id128bc), (LI128, LI128c), (OJ8, OJ8c), (tokhi, tokhi_in), (toklo, toklo_in),
                           (pref, pref_in), (dyn, dyn_in), (bout_sb, bout_in)]:
            sd.inc(s.dma_start(out=dst[tuple(slice(None) for _ in dst.shape)], in_=src), 16)
    SY(f)
    SD_CONST = sd.val

    def f(g):
        for ch in range(4):
            gx[ch].inc(g.dma_start(out=xbf.ap()[1024 * ch:1024 * (ch + 1), :],
                                   in_=xb[1024 * ch:1024 * (ch + 1), :]), 16)
        for dd in range(8):
            g_wr.inc(g.dma_start(out=wr_sb[:, dd, :], in_=wr_in[128 * dd:128 * (dd + 1), :]), 16)
        gp.inc(g.memset(ctxd[:, :, :], 0.0))
    G(f)
    GD_WR, GP_CTXD = g_wr.val, gp.val

    dynv = {}

    for nm, EE in [("t", T), ("v", V), ("sc", SC), ("g", G)]:
        def f(e, nm=nm):
            e.wait_ge(sd.sem, SD_CONST)
            if nm == "sc":
                dynv["hbase_sc"] = e.value_load(dyn[0:1, 2:3], min_val=0, max_val=TOK)
            if nm == "g":
                dynv["hbase_g"] = e.value_load(dyn[0:1, 2:3], min_val=0, max_val=TOK)
                dynv["ctx_g"] = e.value_load(dyn[0:1, 1:2], min_val=0, max_val=(NCORES - 2) * 512)
            if nm == "v":
                dynv["cb_v"] = e.value_load(dyn[0:1, 0:1], min_val=0, max_val=140)
        EE(f)

    def f(g):
        g_xo.inc(g.dma_start(out=xbfo.ap(), in_=xb[ds(dynv["hbase_g"], TOK), :]), 16)
    G(f)
    GD_XO = g_xo.val

    p1 = ExitStack()
    SB1 = lambda n, s, d: p1.enter_context(nc.sbuf_tensor(n, s, d))
    wk_sb = SB1("wk_sb", [128, 8, 512], BF16)
    wv_sb = SB1("wv_sb", [128, 8, 512], BF16)
    ksb = SB1("ksb", [128, 4, NB], BF16)
    etm = SB1("etm", [128, 32, 512], BF16)
    vtm = SB1("vtm", [128, 32, 512], BF16)
    ctxs = SB1("ctxs", [128, 4, 512], BF16)
    xfm1 = [SB1(f"xfm1_{i}", [128, 8, 1024], BF16) for i in range(2)]

    def f(g):
        for dd in range(8):
            g_wkv.inc(g.dma_start(out=wk_sb[:, dd, :], in_=wk_in[128 * dd:128 * (dd + 1), :]), 16)
            g_wkv.inc(g.dma_start(out=wv_sb[:, dd, :], in_=wv_in[128 * dd:128 * (dd + 1), :]), 16)
    G(f)
    GD_WKV = g_wkv.val

    MT1 = NB // 1024
    mm_tile_done = {}
    xfm_wm = {}
    for tt in range(MT1):
        bi = tt % 2

        def f(s, tt=tt, bi=bi):
            s.wait_ge(gx[tt].sem, 16)
            if tt >= 2:
                s.wait_ge(mm.sem, mm_tile_done[tt - 2])
            for dd in range(8):
                xsem[bi].inc(s.dma_start(
                    out=xfm1[bi][:, dd, :],
                    in_=xbf[1024 * tt:1024 * (tt + 1), 128 * dd:128 * (dd + 1)],
                    transpose=True), 16)
        SY(f)
        xfm_wm[tt] = xsem[bi].val

        dvb = dv.val
        acb = ac.val
        kg_wm = []
        vg_wm = []

        def f(t, tt=tt, bi=bi, dvb=dvb, acb=acb):
            if tt == 0:
                t.wait_ge(g_wkv.sem, GD_WKV)
            t.wait_ge(xsem[bi].sem, xfm_wm[tt])
            for gidx in range(8):
                m, sub = gidx // 2, gidx % 2
                bk = gidx % 2
                bw(t, bk)
                for dd in range(8):
                    mi = t.matmul(bank[bk][:, :], wk_sb[:, dd, 128 * m:128 * (m + 1)],
                                  xfm1[bi][:, dd, 512 * sub:512 * (sub + 1)],
                                  start=(dd == 0), stop=(dd == 7))
                mm.inc(mi)
                kg_wm.append(mm.val)
                bank_free[bk] = (dv, dvb + 3 * gidx + 2)
            for c_ in range(8):
                bk = 2 + c_ % 2
                bw(t, bk)
                for dd in range(8):
                    mi = t.matmul(bank[bk][:, :], xfm1[bi][:, dd, 128 * c_:128 * (c_ + 1)],
                                  wv_sb[:, dd, :], start=(dd == 0), stop=(dd == 7))
                mm.inc(mi)
                vg_wm.append(mm.val)
                bank_free[bk] = (ac, acb + c_ + 1)
        T(f)
        mm_tile_done[tt] = mm.val

        def f(v, tt=tt, kg_wm=kg_wm):
            for gidx in range(8):
                m, sub = gidx // 2, gidx % 2
                ts512 = 1024 * tt + 512 * sub
                v.wait_ge(mm.sem, kg_wm[gidx])
                v.wait_ge(dv.sem, dv.val)
                dv.inc(v.tensor_copy(ksb[:, m, ts512:ts512 + 512], bank[gidx % 2][:, :]))
                dv.inc(v.reduce_max(tmpg[:, NG - 1:NG], bank[gidx % 2][:, :], axis=AX.X))
                v.wait_ge(dv.sem, dv.val)
                if tt == 0 and gidx < 2 and sub == 0:
                    dv.inc(v.tensor_copy(m_kv[:, m:m + 1], tmpg[:, NG - 1:NG]))
                elif tt == 0 and sub == 0:
                    dv.inc(v.tensor_copy(m_kv[:, m:m + 1], tmpg[:, NG - 1:NG]))
                else:
                    dv.inc(v.tensor_tensor(m_kv[:, m:m + 1], m_kv[:, m:m + 1],
                                           tmpg[:, NG - 1:NG], A.max))
        V(f)

        def f(sc, tt=tt, vg_wm=vg_wm):
            for c_ in range(8):
                sc.wait_ge(mm.sem, vg_wm[c_])
                ac.inc(sc.copy(vtm[:, 8 * tt + c_, :], bank[2 + c_ % 2][:, :]))
        SC(f)

    DV_KALL, AC_VTM = dv.val, ac.val

    def f(v):
        v.wait_ge(dv.sem, DV_KALL)
        for m in range(4):
            dv.inc(v.tensor_scalar(nm_kv[:, m:m + 1], m_kv[:, m:m + 1], -1.0, None, A.mult))
    V(f)
    DV_NM = dv.val

    def f(sc):
        sc.wait_ge(dv.sem, DV_NM)
        for m in range(4):
            ac.inc(sc.activation(ksb[:, m, :], ksb[:, m, :], AF.Exp,
                                 bias=nm_kv[:, m:m + 1], accum_out=z_kv[:, m:m + 1]))
    SC(f)
    AC_EXP = ac.val

    def f(v):
        v.wait_ge(ac.sem, AC_EXP)
        dv.inc(v.reciprocal(rz_kv[:, :], z_kv[:, :]))
    V(f)
    DV_RZ = dv.val

    etp_wm = []
    dvb_et = dv.val
    acb_et = ac.val

    def f(t, dvb_et=dvb_et, acb_et=acb_et):
        t.wait_ge(ac.sem, AC_EXP)
        gi = 0
        for m in range(4):
            for q4 in range(8):
                bk = 4 + gi % 4
                if gi >= 4:
                    pb = gi - 4
                    if pb % 2 == 0:
                        t.wait_ge(dv.sem, dvb_et + pb // 2 + 1)
                    else:
                        t.wait_ge(ac.sem, acb_et + pb // 2 + 1)
                for i4 in range(4):
                    ng = 4 * q4 + i4
                    mi = t.transpose(bank[bk][:, :].bitcast(BF16)[:, 128 * i4:128 * (i4 + 1)],
                                     ksb[:, m, 128 * ng:128 * (ng + 1)], id128b[:, :])
                mm.inc(mi)
                etp_wm.append(mm.val)
                gi += 1
    T(f)

    def f(v):
        for gi in range(0, 32, 2):
            m, q4 = gi // 8, gi % 8
            v.wait_ge(mm.sem, etp_wm[gi])
            dv.inc(v.tensor_copy(
                etm[:, 4 * q4:4 * (q4 + 1), 128 * m:128 * (m + 1)],
                bank[4 + gi % 4][:, :].bitcast(BF16)[:, 0:512].rearrange("p (a b) -> p a b", a=4)))
    V(f)

    def f(sc):
        for gi in range(1, 32, 2):
            m, q4 = gi // 8, gi % 8
            sc.wait_ge(mm.sem, etp_wm[gi])
            ac.inc(sc.copy(
                etm[:, 4 * q4:4 * (q4 + 1), 128 * m:128 * (m + 1)],
                bank[4 + gi % 4][:, :].bitcast(BF16)[:, 0:512].rearrange("p (a b) -> p a b", a=4)))
    SC(f)
    AC_ETM = ac.val
    DV_ETM = dv.val
    for b_ in range(4):
        bank_free[4 + b_] = (ac, AC_ETM)

    def f(t):
        t.wait_ge(ac.sem, AC_ETM)
        t.wait_ge(dv.sem, DV_ETM)
        t.wait_ge(ac.sem, AC_VTM)
        for m in range(4):
            bw(t, 4 + m)
            for ng in range(32):
                mi = t.matmul(bank[4 + m][:, :], etm[:, ng, 128 * m:128 * (m + 1)],
                              vtm[:, ng, :], start=(ng == 0), stop=(ng == 31))
            mm.inc(mi)
    T(f)
    MM_CTX = mm.val

    def f(v):
        v.wait_ge(mm.sem, MM_CTX)
        v.wait_ge(dv.sem, DV_RZ)
        for m in range(4):
            dv.inc(v.tensor_scalar(ctxs[:, m, :], bank[4 + m][:, :], rz_kv[:, m:m + 1],
                                   None, A.mult))
    V(f)
    DV_CTX = dv.val
    for m in range(4):
        bank_free[4 + m] = (dv, DV_CTX)

    def f(sc):
        sc.wait_ge(dv.sem, DV_CTX)
        for hh in range(8):
            m, p_ = hh // 2, 64 * (hh % 2)
            cd.inc(sc.dma_start(out=ag1_in[64 * hh:64 * (hh + 1), :],
                                in_=ctxs[p_:p_ + 64, m, 64 * hh:64 * (hh + 1)]), 16)
    SC(f)
    CD_AG1 = cd.val

    def f(g):
        g.wait_ge(cd.sem, CD_AG1)
        cc1.inc(g.collective_compute("AllGather", A.bypass, replica_groups=RG,
                                     ins=[ag1_in.ap()], outs=[ag1_out.ap()]), 1)
        g.wait_ge(cc1.sem, 1)
        g.wait_ge(gp.sem, GP_CTXD)
        g.wait_ge(sd.sem, SD_CONST)
        g_ctxa.inc(g.dma_start(
            out=ctxa[:, :, :],
            in_=ag1_out.ap()[ds(dynv["ctx_g"], 1024), :].rearrange("(j p) e -> p j e", p=128)), 16)
    G(f)
    GD_CTXA = g_ctxa.val

    def f(v):
        v.wait_ge(g_ctxa.sem, GD_CTXA)
        v.wait_ge(gp.sem, GP_CTXD)
        for j in range(8):
            dv.inc(v.tensor_copy(ctxd[0:64, j, 0:64], ctxa[0:64, j, :]))
            dv.inc(v.tensor_copy(ctxd[64:128, j, 64:128], ctxa[64:128, j, :]))
    V(f)
    DV_CTXD = dv.val
    GD_CTXD = GD_CTXA

    p1.close()
    pw = ExitStack()
    w1_sb = pw.enter_context(nc.sbuf_tensor("w1_sb", [128, 8, DFF], BF16))
    p2 = ExitStack()
    SB2 = lambda n, s, d: p2.enter_context(nc.sbuf_tensor(n, s, d))
    wq_sb = SB2("wq_sb", [128, 8, D], BF16)
    wo_sb = SB2("wo_sb", [128, 8, D], BF16)
    qsb = SB2("qsb", [128, 8, TT], BF16)
    atn = SB2("atn", [128, 8, TT], BF16)
    aof = SB2("aof", [128, 8, TT], BF16)
    xfm2 = [SB2(f"xfm2_{i}", [128, 8, 1024], BF16) for i in range(2)]
    atm = [SB2(f"atm{i}", [128, D], BF16) for i in range(2)]
    xtm = [SB2(f"xtm{i}", [128, D], F32) for i in range(2)]
    xtb = [SB2(f"xtb{i}", [128, D], BF16) for i in range(2)]
    amat = SB2("amat", [128, NG, 128], BF16)
    ahi = SB2("ahi", [128, NG, 128], BF16)
    alo = SB2("alo", [128, NG, 128], BF16)
    bmat = SB2("bmat", [128, NG, 160], BF16)

    def f(g):
        g.wait_ge(cc1.sem, 1)
        for dd in range(8):
            g_wqo.inc(g.dma_start(out=wq_sb[:, dd, :], in_=wq_in[128 * dd:128 * (dd + 1), :]), 16)
            g_wqo.inc(g.dma_start(out=wo_sb[:, dd, :], in_=wo_in[128 * dd:128 * (dd + 1), :]), 16)
        for dd in range(8):
            for q_ in range(2):
                g_w1.inc(g.dma_start(out=w1_sb[:, dd, 2048 * q_:2048 * (q_ + 1)],
                                     in_=w1_in[128 * dd:128 * (dd + 1), 2048 * q_:2048 * (q_ + 1)]), 16)
    G(f)
    GD_W1 = g_w1.val
    GD_WQO = g_wqo.val

    def f(t):
        t.wait_ge(g_wr.sem, GD_WR)
        bw(t, 6)
        for dd in range(8):
            mi = t.matmul(bank[6][0:1, 0:8], ones128[:, :], wr_sb[:, dd, :],
                          start=(dd == 0), stop=(dd == 7))
        mm.inc(mi)
    T(f)
    WM = mm.val

    def f(v, WM=WM):
        v.wait_ge(mm.sem, WM)
        dv.inc(v.tensor_copy(csr[:, :], bank[6][0:1, 0:8]))
    V(f)
    bank_free[6] = (dv, dv.val)
    DV_CSR = dv.val

    def f(t):
        t.wait_ge(dv.sem, DV_CSR)
        bw(t, 7)
        mm.inc(t.matmul(bank[7][:, 0:8], onesn1[0:1, :], csr[0:1, :], start=True, stop=True))
    T(f)
    WM = mm.val

    def f(v, WM=WM):
        v.wait_ge(mm.sem, WM)
        dv.inc(v.tensor_copy(csn[:, :], bank[7][:, 0:8]))
    V(f)
    bank_free[7] = (dv, dv.val)

    MT2 = TOK // 1024
    mm_u_done = {}
    aoT_cd = {}
    for tt in range(MT2):
        bi = tt % 2

        def f(s, tt=tt, bi=bi):
            if tt == 0:
                s.wait_ge(sd.sem, SD_CONST)
                s.wait_ge(g_xo.sem, GD_XO)
                s.wait_ge(cc1.sem, 1)
            for dd in range(8):
                xsem[bi].inc(s.dma_start(
                    out=xfm2[bi][:, dd, :],
                    in_=xbfo[1024 * tt:1024 * (tt + 1), 128 * dd:128 * (dd + 1)],
                    transpose=True), 16)
        SY(f)
        xfm_wm[(2, tt)] = xsem[bi].val

        for sub in range(2):
            t4 = 2 * tt + sub
            sl = slice(512 * sub, 512 * (sub + 1))

            dvb = dv.val
            q_wm = []

            def f(t, tt=tt, bi=bi, sub=sub, dvb=dvb):
                if tt == 0 and sub == 0:
                    t.wait_ge(g_wqo.sem, GD_WQO)
                if sub == 0:
                    t.wait_ge(xsem[bi].sem, xfm_wm[(2, tt)])
                for m in range(8):
                    bk = m % 2
                    bw(t, bk)
                    for dd in range(8):
                        mi = t.matmul(bank[bk][:, :], wq_sb[:, dd, 128 * m:128 * (m + 1)],
                                      xfm2[bi][:, dd, 512 * sub:512 * (sub + 1)],
                                      start=(dd == 0), stop=(dd == 7))
                    mm.inc(mi)
                    q_wm.append(mm.val)
                    bank_free[bk] = (dv, dvb + m + 1)
            T(f)

            def f(v, q_wm=q_wm):
                for m in range(8):
                    v.wait_ge(mm.sem, q_wm[m])
                    dv.inc(v.tensor_copy(qsb[:, m, :], bank[m % 2][:, :]))
            V(f)
            DV_Q = dv.val

            acb = ac.val
            at_wm = []

            def f(t, tt=tt, sub=sub, DV_Q=DV_Q, acb=acb):
                t.wait_ge(dv.sem, DV_Q)
                if tt == 0 and sub == 0:
                    t.wait_ge(dv.sem, DV_CTXD)
                for j in range(8):
                    bk = 2 + j % 2
                    bw(t, bk)
                    mm.inc(t.matmul(bank[bk][:, :], ctxd[:, j, :], qsb[:, j, :],
                                    start=True, stop=True))
                    at_wm.append(mm.val)
                    bank_free[bk] = (ac, acb + j + 1)
            T(f)

            def f(sc, at_wm=at_wm):
                for j in range(8):
                    sc.wait_ge(mm.sem, at_wm[j])
                    ac.inc(sc.copy(atn[:, j, :], bank[2 + j % 2][:, :]))
            SC(f)
            AC_AT = ac.val

            acb2 = ac.val
            ao_wm = []

            def f(t, AC_AT=AC_AT, acb2=acb2):
                t.wait_ge(ac.sem, AC_AT)
                for o in range(8):
                    bk = 4 + o % 2
                    bw(t, bk)
                    for ad in range(8):
                        mi = t.matmul(bank[bk][:, :], wo_sb[:, ad, 128 * o:128 * (o + 1)],
                                      atn[:, ad, :], start=(ad == 0), stop=(ad == 7))
                    mm.inc(mi)
                    ao_wm.append(mm.val)
                    bank_free[bk] = (ac, acb2 + o + 1)
            T(f)

            cdb_ao = cd.val

            def f(sc, t4=t4, ao_wm=ao_wm, cdb_ao=cdb_ao):
                for o in range(8):
                    sc.wait_ge(mm.sem, ao_wm[o])
                    if t4 >= 1 and o == 0:
                        sc.wait_ge(cd.sem, cdb_ao)
                    ac.inc(sc.activation(aof[:, o, :], bank[4 + o % 2][:, :], AF.Identity,
                                         bias=bout_sb[:, o:o + 1]))
                    sc.wait_ge(ac.sem, ac.val)
                    cd.inc(sc.dma_start(out=aoT[128 * o:128 * (o + 1), TT * t4:TT * (t4 + 1)],
                                        in_=aof[:, o, :]), 16)
            SC(f)
            aoT_cd[t4] = cd.val
            AC_AO = ac.val

            def f(t, tt=tt, bi=bi, sub=sub, AC_AO=AC_AO):
                t.wait_ge(ac.sem, AC_AO)
                bw(t, 6)
                for dd in range(8):
                    t.matmul(bank[6][0:8, :], wr_sb[:, dd, :],
                             xfm2[bi][:, dd, 512 * sub:512 * (sub + 1)],
                             start=(dd == 0), stop=False)
                for dd in range(8):
                    mi = t.matmul(bank[6][0:8, :], wr_sb[:, dd, :], aof[:, dd, :],
                                  start=False, stop=(dd == 7))
                mm.inc(mi)
            T(f)
            mm_u_done[tt] = mm.val
            WM = mm.val

            def f(v, WM=WM):
                v.wait_ge(mm.sem, WM)
                dv.inc(v.tensor_copy(u_T[:, :], bank[6][0:8, :]))
            V(f)
            bank_free[6] = (dv, dv.val)
            DV_UT = dv.val

            def f(t, DV_UT=DV_UT):
                t.wait_ge(dv.sem, DV_UT)
                bw(t, 7)
                for q4 in range(4):
                    mi = t.transpose(bank[7][:, 8 * q4:8 * (q4 + 1)],
                                     u_T[:, 128 * q4:128 * (q4 + 1)], id8[:, :])
                    mm.inc(mi)
            T(f)
            WM = mm.val

            def f(v, t4=t4, WM=WM):
                v.wait_ge(mm.sem, WM)
                for q4 in range(4):
                    dv.inc(v.tensor_copy(lg[:, 4 * t4 + q4, :], bank[7][:, 8 * q4:8 * (q4 + 1)]))
            V(f)
            bank_free[7] = (dv, dv.val)

    DV_LG = dv.val
    CD_AOT = cd.val

    ztile = xtm
    dv_z, cd_zw = {}, {}
    for g_ in range(NG):
        par = g_ % 2

        def f(s, g_=g_, par=par):
            s.wait_ge(cd.sem, aoT_cd[g_ // 4])
            if g_ >= 2:
                s.wait_ge(dv.sem, dv_z[g_ - 2])
            sa[par].inc(s.dma_start(out=atm[par][:, :], in_=aoT[0:D, 128 * g_:128 * (g_ + 1)],
                                    transpose=True), 16)
        SY(f)
        SD_ATM = sa[par].val

        def f(sc, g_=g_, par=par):
            if g_ == 0:
                sc.wait_ge(g_xo.sem, GD_XO)
            if g_ >= 2:
                sc.wait_ge(dv.sem, dv_z[g_ - 2])
            csem[par].inc(sc.dma_start(out=xtb[par][:, :],
                                       in_=xbfo[128 * g_:128 * (g_ + 1), :]), 16)
        SC(f)
        C_X = csem[par].val

        def f(v, g_=g_, par=par, SD_ATM=SD_ATM, C_X=C_X):
            v.wait_ge(sa[par].sem, SD_ATM)
            v.wait_ge(csem[par].sem, C_X)
            if g_ >= 2:
                v.wait_ge(gz[par].sem, cd_zw[g_ - 2])
            dv.inc(v.scalar_tensor_tensor(ztile[par][:, :], atm[par][:, :], 1.0, xtb[par][:, :],
                                          A.mult, A.add, accum_out=zsum[:, g_:g_ + 1]))
            v.wait_ge(dv.sem, dv.val)
            dv.inc(v.scalar_tensor_tensor(zsc[:, :], ztile[par][:, :], 1.0, ztile[par][:, :],
                                          A.mult, A.mult, accum_out=zsq[:, g_:g_ + 1]))
            dv_z[g_] = dv.val
        V(f)

        def f(g, g_=g_, par=par):
            g.wait_ge(dv.sem, dv_z[g_])
            gz[par].inc(g.dma_start(out=zscr[128 * g_:128 * (g_ + 1), :], in_=ztile[par][:, :]), 16)
            cd_zw[g_] = gz[par].val
        G(f)

    def f(v):
        v.wait_ge(dv.sem, dv_z[NG - 1])
        dv.inc(v.tensor_scalar(meanv[:, :], zsum[:, :], 1.0 / D, None, A.mult))
        v.wait_ge(dv.sem, dv.val)
        dv.inc(v.tensor_tensor(tmpg[:, :], meanv[:, :], meanv[:, :], A.mult))
        v.wait_ge(dv.sem, dv.val)
        dv.inc(v.scalar_tensor_tensor(tmpg2[:, :], zsq[:, :], 1.0 / D, tmpg[:, :],
                                      A.mult, A.subtract))
    V(f)
    WM = dv.val

    def f(sc, WM=WM):
        sc.wait_ge(dv.sem, WM)
        ac.inc(sc.activation(tmpg[:, :], tmpg2[:, :], AF.Sqrt, bias=epss[:, 0:1]))
    SC(f)
    WM = ac.val

    def f(v, WM=WM):
        v.wait_ge(ac.sem, WM)
        dv.inc(v.reciprocal(rstd[:, :], tmpg[:, :]))
    V(f)
    DV_RSTD1 = dv.val

    dv_y, cd_y, gd_y = {}, {}, {}
    for g_ in range(NG):
        par = g_ % 2

        def f(g, g_=g_, par=par):
            if g_ == 0:
                g.wait_ge(gz[0].sem, gz[0].val)
                g.wait_ge(gz[1].sem, gz[1].val)
            if g_ >= 2:
                g.wait_ge(dv.sem, dv_y[g_ - 2])
                g.wait_ge(gy[par].sem, gd_y[g_ - 2])
            gsem[par].inc(g.dma_start(out=xtm[par][:, :],
                                      in_=zscr[128 * g_:128 * (g_ + 1), :]), 16)
        G(f)
        C_Z = gsem[par].val

        def f(v, g_=g_, par=par, C_Z=C_Z):
            v.wait_ge(gsem[par].sem, C_Z)
            v.wait_ge(dv.sem, DV_RSTD1)
            if g_ >= 2:
                v.wait_ge(gy[par].sem, gd_y[g_ - 2])
            dv.inc(v.tensor_scalar(ztile[par][:, :], xtm[par][:, :], meanv[:, g_:g_ + 1],
                                   rstd[:, g_:g_ + 1], A.subtract, A.mult))
            dv_y[g_] = dv.val
        V(f)

        def f(g, g_=g_, par=par):
            g.wait_ge(dv.sem, dv_y[g_])
            gy[par].inc(g.dma_start(out=yag_in[128 * g_:128 * (g_ + 1), :], in_=ztile[par][:, :]), 16)
            if stage == "y":
                gy[par].inc(g.dma_start(out=ydbg[128 * g_:128 * (g_ + 1), :], in_=ztile[par][:, :]), 16)
            gd_y[g_] = gy[par].val
        G(f)


    bNG8 = (128, NG, 8)
    r3 = lambda x: x.rearrange("p (g o) -> p g o", o=1)

    def f(v):
        v.wait_ge(dv.sem, DV_LG)
        dv.inc(v.tensor_tensor(lga[:, :, :],
                               csn[:, :].rearrange("p (o e) -> p o e", o=1).broadcast_to(bNG8),
                               r3(meanv[:, :]).broadcast_to(bNG8), A.mult))
        v.wait_ge(dv.sem, dv.val)
        dv.inc(v.tensor_tensor(lga[:, :, :], lga[:, :, :], lg[:, :, :], A.add))
        v.wait_ge(dv.sem, dv.val)
        dv.inc(v.reduce_max(lmax[:, :], lga[:, :, :], axis=AX.X))
        v.wait_ge(dv.sem, dv.val)
        dv.inc(v.tensor_tensor(lgs[:, :, :], lga[:, :, :],
                               r3(lmax[:, :]).broadcast_to(bNG8), A.subtract))
        v.wait_ge(dv.sem, dv.val)
        dv.inc(v.tensor_tensor(lgs[:, :, :], lgs[:, :, :],
                               r3(rstd[:, :]).broadcast_to(bNG8), A.mult))
    V(f)
    WM = dv.val

    def f(sc, WM=WM):
        sc.wait_ge(dv.sem, WM)
        ac.inc(sc.activation(lgs[:, :, :], lgs[:, :, :], AF.Exp))
    SC(f)
    AC_REXP = ac.val

    def f(v):
        v.wait_ge(ac.sem, AC_REXP)
        dv.inc(v.reduce_sum(se[:, :], lgs[:, :, :], axis=AX.X))
        v.wait_ge(dv.sem, dv.val)
        dv.inc(v.reciprocal(pmax[:, :], se[:, :]))
        for g_ in range(NG):
            dv.inc(v.max_index(aidx[:, :], lmax[:, g_:g_ + 1].broadcast_to((128, 8)),
                               lga[:, g_, :]))
            v.wait_ge(dv.sem, dv.val)
            dv.inc(v.tensor_copy(etok[:, g_:g_ + 1], aidx[:, 0:1]))
            v.wait_ge(dv.sem, dv.val)
        dv.inc(v.tensor_tensor(ohbf_all[:, :, :],
                               iota8[:, :].rearrange("p (o e) -> p o e", o=1).broadcast_to(bNG8),
                               r3(etok[:, :]).broadcast_to(bNG8), A.is_equal))
    V(f)
    DV_OH = dv.val

    def f(t):
        t.wait_ge(dv.sem, DV_OH)
        bw(t, 6)
        mm.inc(t.matmul(bank[6][0:1, 0:128], ones128[:, :],
                        ohbf_all[:, :, :].rearrange("p g e -> p (g e)"), start=True, stop=True))
    T(f)
    WM = mm.val

    def f(v, WM=WM):
        v.wait_ge(mm.sem, WM)
        dv.inc(v.tensor_copy(crow[:, :], bank[6][0:1, 0:128]))
    V(f)
    bank_free[6] = (dv, dv.val)
    DV_CROW = dv.val

    def f(sc):
        sc.wait_ge(dv.sem, DV_CROW)
        cd.inc(sc.dma_start(out=crow_h.ap(), in_=crow[0:1, :]), 16)
        sc.wait_ge(cd.sem, cd.val)
        cd.inc(sc.dma_start(out=cnts[:, :],
                            in_=crow_h.ap().rearrange("o (g e) -> (o g) e", g=16)), 16)
    SC(f)
    CD_CNTS = cd.val

    def f(t):
        t.wait_ge(cd.sem, CD_CNTS)
        bw(t, 7)
        mm.inc(t.matmul(bank[7][0:1, 0:8], ones16[:, :], cnts[:, :], start=True, stop=True))
    T(f)
    WM = mm.val

    def f(v, WM=WM):
        v.wait_ge(mm.sem, WM)
        dv.inc(v.tensor_copy(cntrow[:, :], bank[7][0:1, 0:8]))
    V(f)
    bank_free[7] = (dv, dv.val)
    DV_CNTROW = dv.val

    def f(g):
        gp.inc(g.memset(zsc[0:16, :], 0.0))
        g.wait_ge(gp.sem, gp.val)
        g_crow.inc(g.dma_start(out=yag_in[2048:2064, :], in_=zsc[0:16, :]), 16)
        g.wait_ge(g_crow.sem, g_crow.val)
        g.wait_ge(dv.sem, DV_CNTROW)
        g_crow.inc(g.dma_start(out=yag_in[2048:2049, 0:16], in_=cntrow[:, :].bitcast(BF16)), 16)
    G(f)
    GD_CROW = g_crow.val

    def f(g):
        g.wait_ge(gy[0].sem, gy[0].val)
        g.wait_ge(gy[1].sem, gy[1].val)
        g.wait_ge(g_crow.sem, GD_CROW)
        cc2.inc(g.collective_compute("AllGather", A.bypass, replica_groups=RG,
                                     ins=[yag_in.ap()], outs=[yag_out.ap()]), 1)
    G(f)

    def f(g):
        g.wait_ge(cc2.sem, 1)
        g_call.inc(g.dma_start(
            out=call[:, :],
            in_=yag_out.ap().rearrange("(r t) d -> r t d", t=2064)[:, 2048, 0:16].bitcast(F32)), 16)
    G(f)
    GD_CALL = g_call.val

    def f(t):
        t.wait_ge(g_call.sem, GD_CALL)
        bw(t, 6)
        mm.inc(t.matmul(bank[6][0:1, 0:8], pref[:, :], call[:, :], start=True, stop=True))
    T(f)
    WM = mm.val

    def f(v, WM=WM):
        v.wait_ge(mm.sem, WM)
        dv.inc(v.tensor_copy(offs[:, :], bank[6][0:1, 0:8]))
    V(f)
    bank_free[6] = (dv, dv.val)
    DV_OFFS = dv.val

    def f(t):
        t.wait_ge(dv.sem, DV_OFFS)
        t.wait_ge(dv.sem, DV_CROW)
        bw(t, 7)
        mm.inc(t.transpose(bank[7][0:128, 0:1], crow[0:1, :], id1[:, :]))
        mm.inc(t.transpose(bank[7][0:8, 1:2], offs[0:1, :], id1[:, :]))
    T(f)
    WM = mm.val

    def f(v, WM=WM):
        v.wait_ge(mm.sem, WM)
        dv.inc(v.tensor_copy(ccol[:, :], bank[7][0:128, 0:1]))
        dv.inc(v.tensor_copy(ocol[:, :], bank[7][0:8, 1:2]))
    V(f)
    bank_free[7] = (dv, dv.val)
    WM = dv.val

    def f(t, WM=WM):
        t.wait_ge(dv.sem, WM)
        bw(t, 6)
        t.matmul(bank[6][0:128, 0:1], LI128[:, :], ccol[:, :], start=True, stop=False)
        mm.inc(t.matmul(bank[6][0:128, 0:1], OJ8[:, :], ocol[:, :], start=False, stop=True))
    T(f)
    WM = mm.val

    def f(v, WM=WM):
        v.wait_ge(mm.sem, WM)
        dv.inc(v.tensor_copy(carry_col[:, :], bank[6][0:128, 0:1]))
    V(f)
    bank_free[6] = (dv, dv.val)
    WM = dv.val

    def f(t, WM=WM):
        t.wait_ge(dv.sem, WM)
        bw(t, 7)
        mm.inc(t.transpose(bank[7][0:1, 0:128], carry_col[:, :], id128[:, :]))
    T(f)
    WM = mm.val

    def f(v, WM=WM):
        v.wait_ge(mm.sem, WM)
        dv.inc(v.tensor_copy(carry_row[:, :], bank[7][0:1, 0:128]))
    V(f)
    bank_free[7] = (dv, dv.val)
    WM = dv.val

    def f(t, WM=WM):
        t.wait_ge(dv.sem, WM)
        t.wait_ge(dv.sem, DV_OH)
        bw(t, 6)
        t.matmul(bank[6][:, 0:128], trilL[:, :],
                 ohbf_all[:, :, :].rearrange("p g e -> p (g e)"), start=True, stop=False)
        mm.inc(t.matmul(bank[6][:, 0:128], ones1[0:1, :], carry_row[0:1, :],
                        start=False, stop=True))
    T(f)
    WM = mm.val

    def f(v, WM=WM):
        v.wait_ge(mm.sem, WM)
        dv.inc(v.tensor_tensor(posc[:, :, :],
                               bank[6][:, 0:128].rearrange("p (g e) -> p g e", g=NG),
                               ohbf_all[:, :, :], A.mult))
        v.wait_ge(dv.sem, dv.val)
        dv.inc(v.reduce_sum(post[:, :], posc[:, :, :], axis=AX.X))
    V(f)
    bank_free[6] = (dv, dv.val)
    DV_POS = dv.val

    bNG128 = (128, NG, 128)
    bNG160 = (128, NG, 160)

    def f(v):
        v.wait_ge(dv.sem, DV_POS)
        dv.inc(v.tensor_scalar(kept[:, :], post[:, :], 2560.0, None, A.is_lt))
        v.wait_ge(dv.sem, dv.val)
        dv.inc(v.tensor_tensor(kf[:, :], kept[:, :], pmax[:, :], A.mult))
        dv.inc(v.tensor_scalar(k2[:, :], kept[:, :], -1.0, 2.0, A.mult, A.add))
        dv.inc(v.tensor_scalar(tmpg[:, :], post[:, :], 2559.0, None, A.min))
        v.wait_ge(dv.sem, dv.val)
        dv.inc(v.tensor_copy(iscr[:, 3 * NG:4 * NG], tmpg[:, :]))
        v.wait_ge(dv.sem, dv.val)
        dv.inc(v.tensor_scalar(iscr[:, 4 * NG:5 * NG], iscr[:, 3 * NG:4 * NG], 511, None,
                               A.bitwise_and))
        dv.inc(v.tensor_scalar(iscr[:, 3 * NG:4 * NG], iscr[:, 3 * NG:4 * NG], 9, None,
                               A.logical_shift_right))
        v.wait_ge(dv.sem, dv.val)
        dv.inc(v.tensor_copy(tmpg[:, :], iscr[:, 4 * NG:5 * NG]))
        dv.inc(v.tensor_copy(tmpg2[:, :], iscr[:, 3 * NG:4 * NG]))
        v.wait_ge(dv.sem, dv.val)
        dv.inc(v.scalar_tensor_tensor(tmpg[:, :], etok[:, :], 512.0, tmpg[:, :],
                                      A.mult, A.add))
        v.wait_ge(dv.sem, dv.val)
        dv.inc(v.scalar_tensor_tensor(tmpg2[:, :], tmpg2[:, :], 4096.0, tmpg[:, :],
                                      A.mult, A.add))
        v.wait_ge(dv.sem, dv.val)
        dv.inc(v.tensor_copy(growi[:, :], tmpg2[:, :]))
        dv.inc(v.tensor_copy(iscr[:, 0:NG], post[:, :]))
        v.wait_ge(dv.sem, dv.val)
        dv.inc(v.tensor_scalar(iscr[:, NG:2 * NG], iscr[:, 0:NG], 7, None,
                               A.logical_shift_right))
        dv.inc(v.tensor_scalar(iscr[:, 2 * NG:3 * NG], iscr[:, 0:NG], 127, None,
                               A.bitwise_and))
        v.wait_ge(dv.sem, dv.val)
        dv.inc(v.tensor_copy(tmpg[:, :], iscr[:, NG:2 * NG]))
        dv.inc(v.tensor_copy(tmpg2[:, :], iscr[:, 2 * NG:3 * NG]))
        v.wait_ge(dv.sem, dv.val)
        dv.inc(v.scalar_tensor_tensor(tmpg[:, :], etok[:, :], 20.0, tmpg[:, :],
                                      A.mult, A.add))
        v.wait_ge(dv.sem, dv.val)
        dv.inc(v.tensor_tensor(amat[:, :, :],
                               iota128[:, :].rearrange("p (o k) -> p o k", o=1).broadcast_to(bNG128),
                               r3(tmpg2[:, :]).broadcast_to(bNG128), A.is_equal))
        dv.inc(v.tensor_tensor(bmat[:, :, :],
                               iota160[:, :].rearrange("p (o k) -> p o k", o=1).broadcast_to(bNG160),
                               r3(tmpg[:, :]).broadcast_to(bNG160), A.is_equal))
        v.wait_ge(dv.sem, dv.val)
        dv.inc(v.tensor_tensor(tmpg[:, :], kept[:, :], tokhi[:, :], A.mult))
        dv.inc(v.tensor_tensor(tmpg2[:, :], kept[:, :], toklo[:, :], A.mult))
        v.wait_ge(dv.sem, dv.val)
        dv.inc(v.tensor_tensor(ahi[:, :, :], amat[:, :, :],
                               r3(tmpg[:, :]).broadcast_to(bNG128), A.mult))
        dv.inc(v.tensor_tensor(alo[:, :, :], amat[:, :, :],
                               r3(tmpg2[:, :]).broadcast_to(bNG128), A.mult))
    V(f)
    DV_AB = dv.val

    def f(t):
        t.wait_ge(dv.sem, DV_AB)
        bw(t, 4)
        bw(t, 5)
        for g_ in range(NG):
            mi = t.matmul(bank[4][:, 0:160], ahi[:, g_, :], bmat[:, g_, :],
                          start=(g_ == 0), stop=(g_ == NG - 1))
        mm.inc(mi)
        for g_ in range(NG):
            mi = t.matmul(bank[5][:, 0:160], alo[:, g_, :], bmat[:, g_, :],
                          start=(g_ == 0), stop=(g_ == NG - 1))
        mm.inc(mi)
    T(f)
    MM_S2 = mm.val

    def f(v):
        v.wait_ge(mm.sem, MM_S2)
        dv.inc(v.tensor_copy(s2sb[:, :], bank[4][:, 0:160]))
        v.wait_ge(dv.sem, dv.val)
        dv.inc(v.tensor_tensor(s2sb[:, :], s2sb[:, :], bank[5][:, 0:160], A.add))
    V(f)
    bank_free[4] = (dv, dv.val)
    bank_free[5] = (dv, dv.val)
    DV_S2 = dv.val

    def f(sc):
        sc.wait_ge(dv.sem, DV_S2)
        cd.inc(sc.dma_start(out=s2_in.ap(), in_=s2sb[:, :]), 16)
    SC(f)
    CD_S2 = cd.val

    def f(g):
        g.wait_ge(cd.sem, CD_S2)
        cc4.inc(g.collective_compute("AllReduce", A.add, replica_groups=RG,
                                     ins=[s2_in.ap()], outs=[s2_out.ap()]), 1)
        g.wait_ge(cc4.sem, 1)
        g_s2l.inc(g.dma_start(out=s2sb[:, :], in_=s2_out.ap()), 16)
    G(f)
    GD_S2 = g_s2l.val

    def f(v):
        v.wait_ge(g_s2l.sem, GD_S2)
        dv.inc(v.tensor_copy(s2i[:, :], s2sb[:, :]))
        v.wait_ge(dv.sem, dv.val)
        dv.inc(v.tensor_copy(s2my[:, :], s2i[:, ds(dynv["cb_v"], 20)]))
    V(f)
    DV_S2I = dv.val

    p2.close()
    p3s = ExitStack()
    SB3 = lambda n, s, d: p3s.enter_context(nc.sbuf_tensor(n, s, d))
    w2_sb = SB3("w2_sb", [128, 32, D], BF16)
    rows = [SB3(f"rows{i}", [128, 8, TT], BF16) for i in range(2)]
    h_sb = SB3("h_sb", [128, 32, TT], BF16)
    dtile = [SB3(f"dtile{i}", [128, D], BF16) for i in range(2)]
    eotile = [SB3("eotile0", [128, TT], BF16)]
    eotile = [eotile[0], eotile[0]]
    gtile = dtile
    z2 = SB3("z2", [128, D], F32)
    z2all = w1_sb[:, 0:4, :].rearrange("p a (g q) -> p (a g) q", g=4)
    z2b = w1_sb[:, 4:5, :].rearrange("p a b -> p (a b)").bitcast(F32)[:, 0:D]
    ycbb = [SB3("ycbb0", [128, D], BF16),
            w1_sb[:, 5:6, 0:D].rearrange("p a b -> p (a b)"),
            w1_sb[:, 5:6, D:2 * D].rearrange("p a b -> p (a b)"),
            w1_sb[:, 5:6, 2 * D:3 * D].rearrange("p a b -> p (a b)")]
    gtile4 = [w1_sb[:, 6:7, 0:D].rearrange("p a b -> p (a b)"),
              w1_sb[:, 6:7, D:2 * D].rearrange("p a b -> p (a b)"),
              w1_sb[:, 6:7, 2 * D:3 * D].rearrange("p a b -> p (a b)"),
              w1_sb[:, 6:7, 3 * D:4 * D].rearrange("p a b -> p (a b)")]

    def f(g):
        g.wait_ge(dv.sem, DV_S2I)
        for ff in range(32):
            g_w2.inc(g.dma_start(out=w2_sb[:, ff, :], in_=w2_in[128 * ff:128 * (ff + 1), :]), 16)
    G(f)
    GD_W2 = g_w2.val

    sd_disp = {}
    for j in range(CAP // 128):
        def f(g, j=j):
            if j == 0:
                g.wait_ge(dv.sem, DV_S2I)
                g.wait_ge(cc2.sem, 1)
            if j >= 2:
                g.wait_ge(sa[j % 2].sem, sd_disp[j - 2])
            gsem[j % 2].inc(g.indirect_dma_start(
                dtile[j % 2][:, :], None, yag_out.ap(),
                IOA(s2my[:, j:j + 1], 0)), 16)
        G(f)
        WM = gsem[j % 2].val

        def f(s, j=j, WM=WM):
            s.wait_ge(gsem[j % 2].sem, WM)
            sa[j % 2].inc(s.dma_start(out=dispT[128 * j:128 * (j + 1), :], in_=dtile[j % 2][:, :]), 16)
            sd_disp[j] = sa[j % 2].val
        SY(f)
    SD_DISP = sd.val

    NRT = CAP // TT
    mm_row_done = {}
    dtile_last = [None, None]
    for r in range(NRT):
        bi = r % 2

        def f(s, r=r, bi=bi):
            s.wait_ge(sa[0].sem, sd_disp[4 * r + 2])
            s.wait_ge(sa[1].sem, sd_disp[4 * r + 3])
            if r >= 2:
                s.wait_ge(mm.sem, mm_row_done[r - 2])
            for dd in range(8):
                xsem[bi].inc(s.dma_start(
                    out=rows[bi][:, dd, :],
                    in_=dispT[TT * r:TT * (r + 1), 128 * dd:128 * (dd + 1)],
                    transpose=True), 16)
        SY(f)
        xfm_wm[(3, r)] = xsem[bi].val

        acb = ac.val
        h_wm = []

        def f(t, r=r, bi=bi, acb=acb):
            if r == 0:
                t.wait_ge(g_w1.sem, GD_W1)
            t.wait_ge(xsem[bi].sem, xfm_wm[(3, r)])
            for m_ in range(32):
                bk = m_ % 2
                bw(t, bk)
                for dd in range(8):
                    mi = t.matmul(bank[bk][:, :], w1_sb[:, dd, 128 * m_:128 * (m_ + 1)],
                                  rows[bi][:, dd, :], start=(dd == 0), stop=(dd == 7))
                mm.inc(mi)
                h_wm.append(mm.val)
                bank_free[bk] = (ac, acb + m_ + 1)
        T(f)
        MM_HROW = mm.val

        def f(sc, r=r, h_wm=h_wm):
            if r >= 1:
                sc.wait_ge(mm.sem, mm_row_done[r - 1])
            for m_ in range(32):
                sc.wait_ge(mm.sem, h_wm[m_])
                ac.inc(sc.activation(h_sb[:, m_, :], bank[m_ % 2][:, :], AF.Relu))
        SC(f)
        AC_H = ac.val

        dvb = dv.val
        cdb = cd.val
        eo_wm = []

        def f(t, r=r, AC_H=AC_H, dvb=dvb):
            t.wait_ge(ac.sem, AC_H)
            if r == 0:
                t.wait_ge(g_w2.sem, GD_W2)
            for o in range(8):
                bk = 2 + o % 2
                bw(t, bk)
                for ff in range(32):
                    mi = t.matmul(bank[bk][:, :], w2_sb[:, ff, 128 * o:128 * (o + 1)],
                                  h_sb[:, ff, :], start=(ff == 0), stop=(ff == 31))
                mm.inc(mi)
                eo_wm.append(mm.val)
                bank_free[bk] = (dv, dvb + o + 1)
        T(f)
        mm_row_done[r] = mm.val

        def f(v, eo_wm=eo_wm, cdb=cdb):
            for o in range(8):
                v.wait_ge(mm.sem, eo_wm[o])
                if o >= 1:
                    v.wait_ge(cd.sem, cdb + 16 * o)
                dv.inc(v.tensor_copy(eotile[o % 2][:, :], bank[2 + o % 2][:, :]))
        V(f)
        DV_EO = dv.val

        def f(sc, r=r, dvb=dvb):
            for o in range(8):
                sc.wait_ge(dv.sem, dvb + o + 1)
                cd.inc(sc.dma_start(out=eoT[128 * o:128 * (o + 1), TT * r:TT * (r + 1)],
                                    in_=eotile[o % 2][:, :]), 16)
        SC(f)
        CD_EOR = cd.val

        def f(s, r=r, CD_EOR=CD_EOR):
            s.wait_ge(cd.sem, CD_EOR)
            if r == 0:
                s.wait_ge(sa[0].sem, sd_disp[CAP // 128 - 2])
                s.wait_ge(sa[1].sem, sd_disp[CAP // 128 - 1])
            for j in range(4 * r, 4 * r + 4):
                if dtile_last[j % 2] is not None:
                    sm_, v_ = dtile_last[j % 2]
                    s.wait_ge(sm_.sem, v_)
                sme[j % 2].inc(s.dma_start(out=dtile[j % 2][:, :],
                                          in_=eoT[0:D, 128 * j:128 * (j + 1)], transpose=True), 16)
                s.wait_ge(sme[j % 2].sem, sme[j % 2].val)
                smg[j % 2].inc(s.dma_start(out=ag3_in[128 * j:128 * (j + 1), :],
                                           in_=dtile[j % 2][:, :]), 16)
                dtile_last[j % 2] = (smg[j % 2], smg[j % 2].val)
        SY(f)
        AG3W = (smg[0].val, smg[1].val)

        def f(g, r=r, AG3W=AG3W):
            g.wait_ge(smg[0].sem, AG3W[0])
            g.wait_ge(smg[1].sem, AG3W[1])
            cc5.inc(g.collective_compute("AllGather", A.bypass, replica_groups=RG,
                                         ins=[ag3_in.ap()[512 * r:512 * (r + 1), :]],
                                         outs=[ag3_out.ap()[4096 * r:4096 * (r + 1), :]]), 1)
        G(f)
    CD_EOT = cd.val

    z2p = [z2, z2b]
    dv_z2, dv_out, cd_out = {}, {}, {}
    for g_ in range(NG):
        par = g_ % 2
        q4 = g_ % 4

        def f(g, g_=g_, par=par, q4=q4):
            if g_ == 0:
                g.wait_ge(cc5.sem, 5)
            if g_ >= 4:
                g.wait_ge(dv.sem, dv_z2[g_ - 4])
            gq[q4].inc(g.indirect_dma_start(
                gtile4[q4][:, :], None, ag3_out.ap(),
                IOA(growi[:, g_:g_ + 1], 0)), 16)
        G(f)
        G_GAT = gq[q4].val

        def f(sc, g_=g_, par=par, q4=q4):
            if g_ >= 4:
                sc.wait_ge(dv.sem, dv_z2[g_ - 4])
            cq[q4].inc(sc.dma_start(out=ycbb[q4][:, :],
                                     in_=yag_in[128 * g_:128 * (g_ + 1), :]), 16)
        SC(f)
        C_Y = cq[q4].val

        def f(v, g_=g_, par=par, q4=q4, G_GAT=G_GAT, C_Y=C_Y):
            v.wait_ge(gq[q4].sem, G_GAT)
            v.wait_ge(cq[q4].sem, C_Y)
            if g_ >= 2:
                v.wait_ge(dv.sem, dv_z2[g_ - 2])
            dv.inc(v.tensor_scalar(z2p[par][:, :], gtile4[q4][:, :], kf[:, g_:g_ + 1], None, A.mult))
            v.wait_ge(dv.sem, dv.val)
            dv.inc(v.scalar_tensor_tensor(z2p[par][:, :], ycbb[q4][:, :], k2[:, g_:g_ + 1],
                                          z2p[par][:, :], A.mult, A.add,
                                          accum_out=zsum[:, g_:g_ + 1]))
            v.wait_ge(dv.sem, dv.val)
            dv.inc(v.tensor_copy(z2all[:, g_:g_ + 1, :].rearrange("p a b -> p (a b)"), z2p[par][:, :]))
            v.wait_ge(dv.sem, dv.val)
            dv.inc(v.scalar_tensor_tensor(zsc[:, :], z2p[par][:, :], 1.0, z2p[par][:, :],
                                          A.mult, A.mult, accum_out=zsq[:, g_:g_ + 1]))
            dv_z2[g_] = dv.val
        V(f)

    def f(v):
        v.wait_ge(dv.sem, dv_z2[NG - 1])
        dv.inc(v.tensor_scalar(meanv[:, :], zsum[:, :], 1.0 / D, None, A.mult))
        v.wait_ge(dv.sem, dv.val)
        dv.inc(v.tensor_tensor(tmpg[:, :], meanv[:, :], meanv[:, :], A.mult))
        v.wait_ge(dv.sem, dv.val)
        dv.inc(v.scalar_tensor_tensor(tmpg2[:, :], zsq[:, :], 1.0 / D, tmpg[:, :],
                                      A.mult, A.subtract))
    V(f)
    WM = dv.val

    def f(sc, WM=WM):
        sc.wait_ge(dv.sem, WM)
        ac.inc(sc.activation(tmpg[:, :], tmpg2[:, :], AF.Sqrt, bias=epss[:, 0:1]))
    SC(f)
    WM = ac.val

    def f(v, WM=WM):
        v.wait_ge(ac.sem, WM)
        dv.inc(v.reciprocal(rstd[:, :], tmpg[:, :]))
    V(f)
    DV_RSTD2 = dv.val

    zpar = [z2, z2b]
    for g_ in range(NG):
        par = g_ % 2

        def f(v, g_=g_, par=par):
            v.wait_ge(dv.sem, DV_RSTD2)
            if g_ >= 2:
                v.wait_ge(cdo[par].sem, cd_out[g_ - 2])
            dv.inc(v.tensor_scalar(zpar[par][:, :], z2all[:, g_:g_ + 1, :].rearrange("p a b -> p (a b)"), meanv[:, g_:g_ + 1],
                                   rstd[:, g_:g_ + 1], A.subtract, A.mult))
            dv_out[g_] = dv.val
        V(f)

        def f(sc, g_=g_, par=par):
            sc.wait_ge(dv.sem, dv_out[g_])
            cdo[par].inc(sc.dma_start(out=out_ext[128 * g_:128 * (g_ + 1), :], in_=zpar[par][:, :]), 16)
            cd_out[g_] = cdo[par].val
        SC(f)

    def f(sc):
        sc.wait_ge(cdo[0].sem, cdo[0].val)
        sc.wait_ge(cdo[1].sem, cdo[1].val)
        sc.wait_ge(cd.sem, cd.val)
        sc.wait_ge(cd2.sem, cd2.val)
        sc.wait_ge(cd3.sem, cd3.val)
    SC(f)

    def f(s):
        s.wait_ge(sd.sem, sd.val)
        s.wait_ge(sd2.sem, sd2.val)
        s.wait_ge(sd3.sem, sd3.val)
    SY(f)

    def f(g):
        for s_ in [gd, g2, g3, g_wr, g_xo, g_wkv, g_wqo, g_w1, g_w2, g_ctxa,
                   g_call, g_s2l, g_crow, g_z2, gq[0], gq[1], gq[2], gq[3],
                   gy[0], gy[1], gz[0], gz[1],
                   sa[0], sa[1], sme[0], sme[1], smg[0], smg[1]] + gx:
            g.wait_ge(s_.sem, s_.val)
    G(f)

    p3s.close()
    pw.close()
    ctx.close()
    return nc


def make_inmaps(x, W_qkv, W_out, b_out, Wr, W1, W2):
    x = np.asarray(x)
    in_maps = []
    for c in range(NCORES):
        b, h = c // 2, c % 2
        tok0 = np.arange(128, dtype=np.float32)[:, None] + 128.0 * np.arange(NG, dtype=np.float32)[None, :]
        tokid = tok0 + 2064.0 * c
        in_maps.append({
            "xb": np.ascontiguousarray(x.reshape(4, NB, D)[b]),
            "wk": np.ascontiguousarray(W_qkv[:, D + 512 * h: D + 512 * (h + 1)]),
            "wv": np.ascontiguousarray(W_qkv[:, 2 * D + 512 * h: 2 * D + 512 * (h + 1)]),
            "wq": np.ascontiguousarray(W_qkv[:, 0:D]),
            "wo": np.ascontiguousarray(W_out),
            "bout2": np.ascontiguousarray(np.asarray(b_out).reshape(8, 128).T),
            "wr": np.ascontiguousarray(Wr),
            "w1e": np.ascontiguousarray(W1[c]),
            "w2e": np.ascontiguousarray(W2[c]),
            "tokhi": np.floor(tokid / 128.0) * 128.0,
            "toklo": tokid - np.floor(tokid / 128.0) * 128.0,
            "prefmask": (np.arange(E) < c).astype(np.float32).reshape(E, 1),
            "dyn": np.array([[20 * c, 512 * 2 * b, 2048 * h, 0]], dtype=np.int32),
        })
    return in_maps


_NC_CACHE = {}


def _get_nc():
    if "nc" not in _NC_CACHE:
        from concourse.library_overlay import lower_extended_insts
        nc = build_nc(stage="full")
        lower_extended_insts(nc)
        _NC_CACHE["nc"] = nc
    return _NC_CACHE["nc"]


def kernel(L0_attn_x, L0_attn_W_qkv, L0_attn_W_out, L0_attn_b_out,
           L0_ffn_Wr, L0_ffn_W1, L0_ffn_W2):
    from concourse.bass_utils import run_bass_kernel_spmd
    x = np.ascontiguousarray(np.asarray(L0_attn_x), dtype=np.float32)
    nc = _get_nc()
    in_maps = make_inmaps(x.reshape(-1, D), np.asarray(L0_attn_W_qkv),
                          np.asarray(L0_attn_W_out), np.asarray(L0_attn_b_out),
                          np.asarray(L0_ffn_Wr), np.asarray(L0_ffn_W1),
                          np.asarray(L0_ffn_W2))
    res = run_bass_kernel_spmd(nc, in_maps, core_ids=list(range(NCORES)))
    out = np.concatenate([r["out"] for r in res.results], axis=0)
    return out.reshape(4, NB, D)


# version 09
# speedup vs baseline: 1.0067x; 1.0067x over previous
import numpy as np
import ml_dtypes
from contextlib import ExitStack

import concourse.bass as bass
import concourse.mybir as mybir

F32 = mybir.dt.float32
BF16 = mybir.dt.bfloat16
I32 = mybir.dt.int32
U32 = mybir.dt.uint32
A = mybir.AluOpType
AF = mybir.ActivationFunctionType
AX = mybir.AxisListType
ds = bass.ds
IOA = bass.IndirectOffsetOnAxis
bf = ml_dtypes.bfloat16

NCORES, TOK, NB, D, E, CAP, DFF, TT = 8, 2048, 4096, 1024, 8, 2560, 4096, 512
NG = TOK // 128
EPS = 1e-5
RG = [list(range(NCORES))]


class Ctr:
    def __init__(self, sem):
        self.sem, self.val = sem, 0

    def inc(self, binst, n=1):
        self.val += n
        binst.then_inc(self.sem, n)
        return binst


def build_nc(stage="full"):
    nc = bass.Bass("TRN2", target_bir_lowering=False)

    xb = nc.dram_tensor("xb", [NB, D], F32, kind="ExternalInput").ap()
    wk_in = nc.dram_tensor("wk", [D, 512], F32, kind="ExternalInput").ap()
    wv_in = nc.dram_tensor("wv", [D, 512], F32, kind="ExternalInput").ap()
    wq_in = nc.dram_tensor("wq", [D, D], F32, kind="ExternalInput").ap()
    wo_in = nc.dram_tensor("wo", [D, D], F32, kind="ExternalInput").ap()
    bout_in = nc.dram_tensor("bout2", [128, 8], F32, kind="ExternalInput").ap()
    wr_in = nc.dram_tensor("wr", [D, E], F32, kind="ExternalInput").ap()
    w1_in = nc.dram_tensor("w1e", [D, DFF], F32, kind="ExternalInput").ap()
    w2_in = nc.dram_tensor("w2e", [DFF, D], F32, kind="ExternalInput").ap()
    tokhi_in = nc.dram_tensor("tokhi", [128, NG], F32, kind="ExternalInput").ap()
    toklo_in = nc.dram_tensor("toklo", [128, NG], F32, kind="ExternalInput").ap()
    pref_in = nc.dram_tensor("prefmask", [E, 1], F32, kind="ExternalInput").ap()
    dyn_in = nc.dram_tensor("dyn", [1, 4], I32, kind="ExternalInput").ap()
    out_ext = nc.dram_tensor("out", [TOK, D], F32, kind="ExternalOutput").ap()
    ydbg = nc.dram_tensor("ydbg", [TOK, D], F32, kind="ExternalOutput").ap() if stage == "y" else None

    IT = lambda a, n: nc.inline_tensor(a, name=n).ap()
    ident8 = IT(np.eye(8, dtype=np.float32), "ident8")
    trilLc = IT((np.arange(128)[:, None] < np.arange(128)[None, :]).astype(bf), "trilLc")
    tril16c = IT((np.arange(16)[:, None] < np.arange(16)[None, :]).astype(np.float32), "tril16c")
    ones128c = IT(np.ones((128, 1), dtype=bf), "ones128c")
    ones16c = IT(np.ones((16, 1), dtype=np.float32), "ones16c")
    epsc = IT(np.full((128, 1), EPS, dtype=np.float32), "epsc")
    id1c = IT(np.eye(1, dtype=np.float32), "id1c")
    id128c = IT(np.eye(128, dtype=np.float32), "id128c")
    id128bc = IT(np.eye(128).astype(bf), "id128bc")
    _g = np.arange(128) // 8
    _j = np.arange(128) % 8
    LI128c = IT(((_g[:, None] < _g[None, :]) & (_j[:, None] == _j[None, :])).astype(np.float32), "LI128c")
    OJ8c = IT((np.arange(8)[:, None] == _j[None, :]).astype(np.float32), "OJ8c")
    ones1c = IT(np.ones((1, 128), dtype=np.float32), "ones1c")
    onesn1c = IT(np.full((1, 128), -1.0, dtype=np.float32), "onesn1c")
    iota8c = IT(np.tile(np.arange(8, dtype=np.float32), (128, 1)), "iota8c")
    iota128c = IT(np.tile(np.arange(128, dtype=np.float32), (128, 1)), "iota128c")
    iota160c = IT(np.tile(np.arange(160, dtype=np.float32), (128, 1)), "iota160c")

    DT = nc.dram_tensor
    xbf = DT("xbf", [NB, D], BF16)
    xbfo = DT("xbfo", [TOK, D], BF16)
    eT = DT("eT", [512, NB], BF16)
    aoT = DT("aoT", [D, TOK], BF16)
    ag1_in = DT("ag1_in", [512, 64], BF16)
    ag1_out = DT("ag1_out", [NCORES * 512, 64], BF16, addr_space="Shared")
    yag_in = DT("yag_in", [2064, D], BF16)
    yag_out = DT("yag_out", [NCORES * 2064, D], BF16, addr_space="Shared")
    zscr = DT("zscr", [TOK, D], BF16)
    z2scr = DT("z2scr", [TOK, D], BF16)
    cnt_in = DT("cnt_in", [1, E], F32)
    cnt_out = DT("cnt_out", [NCORES, E], F32, addr_space="Shared")
    crow_h = DT("crow_h", [1, 128], F32)
    s2_in = DT("s2_in", [128, 160], F32)
    s2_out = DT("s2_out", [128, 160], F32, addr_space="Shared")
    dispT = DT("dispT", [CAP, D], BF16)
    eoT = DT("eoT", [D, CAP], BF16)
    ag3_in = DT("ag3_in", [CAP, D], BF16)
    ag3_out = DT("ag3_out", [NCORES * CAP, D], BF16, addr_space="Shared")

    ctx = ExitStack()
    SBUF = lambda n, s, d, **kw: ctx.enter_context(nc.sbuf_tensor(n, s, d, **kw))
    SEM = lambda n: Ctr(ctx.enter_context(nc.semaphore(n)))
    bank = [ctx.enter_context(nc.psum_tensor(f"bank{i}", [128, 512], F32)) for i in range(8)]
    bank_free = [None] * 8

    def bw(t, i):
        if bank_free[i] is not None:
            c, v = bank_free[i]
            t.wait_ge(c.sem, v)

    sd = SEM("sd"); sd2 = SEM("sd2"); sd3 = SEM("sd3")
    cd = SEM("cd"); cd2 = SEM("cd2"); cd3 = SEM("cd3")
    gd = SEM("gd"); g2 = SEM("g2"); g3 = SEM("g3")
    mm = SEM("mm"); dv = SEM("dv"); ac = SEM("ac"); gp = SEM("gp")
    cc1, cc2, cc3, cc4, cc5 = [SEM(f"cc{i}") for i in range(1, 6)]
    gx = [SEM(f"gx{i}") for i in range(4)]
    g_wr = SEM("g_wr"); g_xo = SEM("g_xo"); g_wkv = SEM("g_wkv")
    g_wqo = SEM("g_wqo"); g_w1 = SEM("g_w1"); g_w2 = SEM("g_w2")
    g_ctxa = SEM("g_ctxa"); g_call = SEM("g_call"); g_s2l = SEM("g_s2l")
    g_crow = SEM("g_crow"); g_z2 = SEM("g_z2")
    cdo = [SEM("cdo0"), SEM("cdo1")]
    warm = SEM("warm")
    gq = [SEM(f"gq{i}") for i in range(4)]
    cq = [SEM(f"cq{i}") for i in range(4)]
    gy = [SEM("gy0"), SEM("gy1")]
    sa = [SEM("sa0"), SEM("sa1")]
    sme = [SEM("sme0"), SEM("sme1")]
    smg = [SEM("smg0"), SEM("smg1")]
    gz = [SEM("gz0"), SEM("gz1")]
    gsem = [g2, g3]
    xsem = [sd2, sd3]
    csem = [cd2, cd3]

    m_kv = SBUF("m_kv", [128, 4], F32); nm_kv = SBUF("nm_kv", [128, 4], F32)
    z_kv = SBUF("z_kv", [128, 4], F32); rz_kv = SBUF("rz_kv", [128, 4], F32)
    ctxd = SBUF("ctxd", [128, 8, 128], BF16)
    ctxa = SBUF("ctxa", [128, 8, 64], BF16)
    bout_sb = SBUF("bout_sb", [128, 8], F32)
    wr_sb = SBUF("wr_sb", [128, 8, 8], BF16)
    csn = SBUF("csn", [128, 8], F32); csr = SBUF("csr", [1, 8], F32)
    iota8 = SBUF("iota8", [128, 8], F32)
    iota128 = SBUF("iota128", [128, 128], F32)
    iota160 = SBUF("iota160", [128, 160], F32)
    trilL = SBUF("trilL", [128, 128], BF16)
    tril16 = SBUF("tril16", [16, 16], F32)
    ones128 = SBUF("ones128", [128, 1], BF16)
    ones16 = SBUF("ones16", [16, 1], F32)
    epss = SBUF("epss", [128, 1], F32)
    ones1 = SBUF("ones1", [1, 128], F32)
    onesn1 = SBUF("onesn1", [1, 128], F32)
    id8 = SBUF("id8", [8, 8], F32)
    id1 = SBUF("id1", [1, 1], F32)
    id128 = SBUF("id128", [128, 128], F32)
    id128b = SBUF("id128b", [128, 128], BF16)
    LI128 = SBUF("LI128", [128, 128], F32)
    OJ8 = SBUF("OJ8", [8, 128], F32)
    ccol = SBUF("ccol", [128, 1], F32)
    ocol = SBUF("ocol", [8, 1], F32)
    carry_col = SBUF("carry_col", [128, 1], F32)
    carry_row = SBUF("carry_row", [1, 128], F32)
    posc = SBUF("posc", [128, NG, 8], F32)
    tokhi = SBUF("tokhi_s", [128, NG], F32); toklo = SBUF("toklo_s", [128, NG], F32)
    pref = SBUF("pref_s", [E, 1], F32)
    dyn = SBUF("dyn_s", [1, 4], I32)
    pmax = SBUF("pmax", [128, NG], F32); etok = SBUF("etok", [128, NG], F32)
    post = SBUF("post", [128, NG], F32); kept = SBUF("kept", [128, NG], F32)
    kf = SBUF("kf", [128, NG], F32); k2 = SBUF("k2", [128, NG], F32)
    growi = SBUF("growi", [128, NG], I32)
    zsum = SBUF("zsum", [128, NG], F32); zsq = SBUF("zsq", [128, NG], F32)
    meanv = SBUF("meanv", [128, NG], F32); rstd = SBUF("rstd", [128, NG], F32)
    lg = SBUF("lg", [128, NG, 8], F32)
    lga = SBUF("lga", [128, NG, 8], F32)
    lgs = SBUF("lgs", [128, NG, 8], F32)
    lmax = SBUF("lmax", [128, NG], F32); se = SBUF("se", [128, NG], F32)
    tmpg = SBUF("tmpg", [128, NG], F32); tmpg2 = SBUF("tmpg2", [128, NG], F32)
    ohbf_all = SBUF("ohbf_all", [128, NG, 8], BF16)
    aidx = SBUF("aidx", [128, 8], U32)
    cnts = SBUF("cnts", [16, 8], F32); crow = SBUF("crow", [1, 128], F32)
    cntrow = SBUF("cntrow", [1, 8], F32)
    call = SBUF("call", [NCORES, 8], F32)
    offs = SBUF("offs", [1, 8], F32); goff = SBUF("goff", [16, 8], F32)
    u_T = SBUF("u_T", [8, 512], F32)
    zsc = SBUF("zsc", [128, D], BF16)
    s2sb = SBUF("s2sb", [128, 160], F32)
    s2i = SBUF("s2i", [128, 160], I32)
    s2my = SBUF("s2my", [128, 20], I32)
    iscr = SBUF("iscr", [128, 5 * NG], I32)
    rkps = SBUF("rkps", [128, 8], F32)

    block = ctx.enter_context(nc.Block())
    SY, T, V, SC, G = block.sync, block.tensor, block.vector, block.scalar, block.gpsimd

    def f(s):
        for (dst, src) in [(iota8, iota8c), (iota128, iota128c), (iota160, iota160c),
                           (trilL, trilLc), (tril16, tril16c), (ones128, ones128c),
                           (ones16, ones16c), (ones1, ones1c), (onesn1, onesn1c), (epss, epsc),
                           (id8, ident8), (id1, id1c), (id128, id128c), (id128b, id128bc), (LI128, LI128c), (OJ8, OJ8c), (tokhi, tokhi_in), (toklo, toklo_in),
                           (pref, pref_in), (dyn, dyn_in), (bout_sb, bout_in)]:
            sd.inc(s.dma_start(out=dst[tuple(slice(None) for _ in dst.shape)], in_=src), 16)
    SY(f)
    SD_CONST = sd.val

    def f(g):
        for ch in range(4):
            gx[ch].inc(g.dma_start(out=xbf.ap()[1024 * ch:1024 * (ch + 1), :],
                                   in_=xb[1024 * ch:1024 * (ch + 1), :]), 16)
        for dd in range(8):
            g_wr.inc(g.dma_start(out=wr_sb[:, dd, :], in_=wr_in[128 * dd:128 * (dd + 1), :]), 16)
        gp.inc(g.memset(ctxd[:, :, :], 0.0))
    G(f)
    GD_WR, GP_CTXD = g_wr.val, gp.val

    dynv = {}

    for nm, EE in [("t", T), ("v", V), ("sc", SC), ("g", G)]:
        def f(e, nm=nm):
            e.wait_ge(sd.sem, SD_CONST)
            if nm == "sc":
                dynv["hbase_sc"] = e.value_load(dyn[0:1, 2:3], min_val=0, max_val=TOK)
            if nm == "g":
                dynv["hbase_g"] = e.value_load(dyn[0:1, 2:3], min_val=0, max_val=TOK)
                dynv["ctx_g"] = e.value_load(dyn[0:1, 1:2], min_val=0, max_val=(NCORES - 2) * 512)
            if nm == "v":
                dynv["cb_v"] = e.value_load(dyn[0:1, 0:1], min_val=0, max_val=140)
        EE(f)

    def f(g):
        g_xo.inc(g.dma_start(out=xbfo.ap(), in_=xb[ds(dynv["hbase_g"], TOK), :]), 16)
    G(f)
    GD_XO = g_xo.val

    def f(t):
        for w_ in range(1152):
            mi = t.matmul(bank[7][:, 0:128], id128b[:, :], id128b[:, :],
                          start=True, stop=True)
        warm.inc(mi)
    T(f)

    p1 = ExitStack()
    SB1 = lambda n, s, d: p1.enter_context(nc.sbuf_tensor(n, s, d))
    wk_sb = SB1("wk_sb", [128, 8, 512], BF16)
    wv_sb = SB1("wv_sb", [128, 8, 512], BF16)
    ksb = SB1("ksb", [128, 4, NB], BF16)
    etm = SB1("etm", [128, 32, 512], BF16)
    vtm = SB1("vtm", [128, 32, 512], BF16)
    ctxs = SB1("ctxs", [128, 4, 512], BF16)
    xfm1 = [SB1(f"xfm1_{i}", [128, 8, 1024], BF16) for i in range(2)]

    def f(g):
        for dd in range(8):
            g_wkv.inc(g.dma_start(out=wk_sb[:, dd, :], in_=wk_in[128 * dd:128 * (dd + 1), :]), 16)
            g_wkv.inc(g.dma_start(out=wv_sb[:, dd, :], in_=wv_in[128 * dd:128 * (dd + 1), :]), 16)
    G(f)
    GD_WKV = g_wkv.val

    MT1 = NB // 1024
    mm_tile_done = {}
    xfm_wm = {}
    for tt in range(MT1):
        bi = tt % 2

        def f(s, tt=tt, bi=bi):
            s.wait_ge(gx[tt].sem, 16)
            if tt >= 2:
                s.wait_ge(mm.sem, mm_tile_done[tt - 2])
            for dd in range(8):
                xsem[bi].inc(s.dma_start(
                    out=xfm1[bi][:, dd, :],
                    in_=xbf[1024 * tt:1024 * (tt + 1), 128 * dd:128 * (dd + 1)],
                    transpose=True), 16)
        SY(f)
        xfm_wm[tt] = xsem[bi].val

        dvb = dv.val
        acb = ac.val
        kg_wm = []
        vg_wm = []

        def f(t, tt=tt, bi=bi, dvb=dvb, acb=acb):
            if tt == 0:
                t.wait_ge(g_wkv.sem, GD_WKV)
                t.wait_ge(warm.sem, 1)
            t.wait_ge(xsem[bi].sem, xfm_wm[tt])
            for gidx in range(8):
                m, sub = gidx // 2, gidx % 2
                bk = gidx % 2
                bw(t, bk)
                for dd in range(8):
                    mi = t.matmul(bank[bk][:, :], wk_sb[:, dd, 128 * m:128 * (m + 1)],
                                  xfm1[bi][:, dd, 512 * sub:512 * (sub + 1)],
                                  start=(dd == 0), stop=(dd == 7))
                mm.inc(mi)
                kg_wm.append(mm.val)
                bank_free[bk] = (dv, dvb + 3 * gidx + 2)
            for c_ in range(8):
                bk = 2 + c_ % 2
                bw(t, bk)
                for dd in range(8):
                    mi = t.matmul(bank[bk][:, :], xfm1[bi][:, dd, 128 * c_:128 * (c_ + 1)],
                                  wv_sb[:, dd, :], start=(dd == 0), stop=(dd == 7))
                mm.inc(mi)
                vg_wm.append(mm.val)
                bank_free[bk] = (ac, acb + c_ + 1)
        T(f)
        mm_tile_done[tt] = mm.val

        def f(v, tt=tt, kg_wm=kg_wm):
            for gidx in range(8):
                m, sub = gidx // 2, gidx % 2
                ts512 = 1024 * tt + 512 * sub
                v.wait_ge(mm.sem, kg_wm[gidx])
                v.wait_ge(dv.sem, dv.val)
                dv.inc(v.tensor_copy(ksb[:, m, ts512:ts512 + 512], bank[gidx % 2][:, :]))
                dv.inc(v.reduce_max(tmpg[:, NG - 1:NG], bank[gidx % 2][:, :], axis=AX.X))
                v.wait_ge(dv.sem, dv.val)
                if tt == 0 and gidx < 2 and sub == 0:
                    dv.inc(v.tensor_copy(m_kv[:, m:m + 1], tmpg[:, NG - 1:NG]))
                elif tt == 0 and sub == 0:
                    dv.inc(v.tensor_copy(m_kv[:, m:m + 1], tmpg[:, NG - 1:NG]))
                else:
                    dv.inc(v.tensor_tensor(m_kv[:, m:m + 1], m_kv[:, m:m + 1],
                                           tmpg[:, NG - 1:NG], A.max))
        V(f)

        def f(sc, tt=tt, vg_wm=vg_wm):
            for c_ in range(8):
                sc.wait_ge(mm.sem, vg_wm[c_])
                ac.inc(sc.copy(vtm[:, 8 * tt + c_, :], bank[2 + c_ % 2][:, :]))
        SC(f)

    DV_KALL, AC_VTM = dv.val, ac.val

    def f(v):
        v.wait_ge(dv.sem, DV_KALL)
        for m in range(4):
            dv.inc(v.tensor_scalar(nm_kv[:, m:m + 1], m_kv[:, m:m + 1], -1.0, None, A.mult))
    V(f)
    DV_NM = dv.val

    def f(sc):
        sc.wait_ge(dv.sem, DV_NM)
        for m in range(4):
            ac.inc(sc.activation(ksb[:, m, :], ksb[:, m, :], AF.Exp,
                                 bias=nm_kv[:, m:m + 1], accum_out=z_kv[:, m:m + 1]))
    SC(f)
    AC_EXP = ac.val

    def f(v):
        v.wait_ge(ac.sem, AC_EXP)
        dv.inc(v.reciprocal(rz_kv[:, :], z_kv[:, :]))
    V(f)
    DV_RZ = dv.val

    etp_wm = []
    dvb_et = dv.val
    acb_et = ac.val

    def f(t, dvb_et=dvb_et, acb_et=acb_et):
        t.wait_ge(ac.sem, AC_EXP)
        gi = 0
        for m in range(4):
            for q4 in range(8):
                bk = 4 + gi % 4
                if gi >= 4:
                    pb = gi - 4
                    if pb % 2 == 0:
                        t.wait_ge(dv.sem, dvb_et + pb // 2 + 1)
                    else:
                        t.wait_ge(ac.sem, acb_et + pb // 2 + 1)
                for i4 in range(4):
                    ng = 4 * q4 + i4
                    mi = t.transpose(bank[bk][:, :].bitcast(BF16)[:, 128 * i4:128 * (i4 + 1)],
                                     ksb[:, m, 128 * ng:128 * (ng + 1)], id128b[:, :])
                mm.inc(mi)
                etp_wm.append(mm.val)
                gi += 1
    T(f)

    def f(v):
        for gi in range(0, 32, 2):
            m, q4 = gi // 8, gi % 8
            v.wait_ge(mm.sem, etp_wm[gi])
            dv.inc(v.tensor_copy(
                etm[:, 4 * q4:4 * (q4 + 1), 128 * m:128 * (m + 1)],
                bank[4 + gi % 4][:, :].bitcast(BF16)[:, 0:512].rearrange("p (a b) -> p a b", a=4)))
    V(f)

    def f(sc):
        for gi in range(1, 32, 2):
            m, q4 = gi // 8, gi % 8
            sc.wait_ge(mm.sem, etp_wm[gi])
            ac.inc(sc.copy(
                etm[:, 4 * q4:4 * (q4 + 1), 128 * m:128 * (m + 1)],
                bank[4 + gi % 4][:, :].bitcast(BF16)[:, 0:512].rearrange("p (a b) -> p a b", a=4)))
    SC(f)
    AC_ETM = ac.val
    DV_ETM = dv.val
    for b_ in range(4):
        bank_free[4 + b_] = (ac, AC_ETM)

    def f(t):
        t.wait_ge(ac.sem, AC_ETM)
        t.wait_ge(dv.sem, DV_ETM)
        t.wait_ge(ac.sem, AC_VTM)
        for m in range(4):
            bw(t, 4 + m)
            for ng in range(32):
                mi = t.matmul(bank[4 + m][:, :], etm[:, ng, 128 * m:128 * (m + 1)],
                              vtm[:, ng, :], start=(ng == 0), stop=(ng == 31))
            mm.inc(mi)
    T(f)
    MM_CTX = mm.val

    def f(v):
        v.wait_ge(mm.sem, MM_CTX)
        v.wait_ge(dv.sem, DV_RZ)
        for m in range(4):
            dv.inc(v.tensor_scalar(ctxs[:, m, :], bank[4 + m][:, :], rz_kv[:, m:m + 1],
                                   None, A.mult))
    V(f)
    DV_CTX = dv.val
    for m in range(4):
        bank_free[4 + m] = (dv, DV_CTX)

    def f(sc):
        sc.wait_ge(dv.sem, DV_CTX)
        for hh in range(8):
            m, p_ = hh // 2, 64 * (hh % 2)
            cd.inc(sc.dma_start(out=ag1_in[64 * hh:64 * (hh + 1), :],
                                in_=ctxs[p_:p_ + 64, m, 64 * hh:64 * (hh + 1)]), 16)
    SC(f)
    CD_AG1 = cd.val

    def f(g):
        g.wait_ge(cd.sem, CD_AG1)
        cc1.inc(g.collective_compute("AllGather", A.bypass, replica_groups=RG,
                                     ins=[ag1_in.ap()], outs=[ag1_out.ap()]), 1)
        g.wait_ge(cc1.sem, 1)
        g.wait_ge(gp.sem, GP_CTXD)
        g.wait_ge(sd.sem, SD_CONST)
        g_ctxa.inc(g.dma_start(
            out=ctxa[:, :, :],
            in_=ag1_out.ap()[ds(dynv["ctx_g"], 1024), :].rearrange("(j p) e -> p j e", p=128)), 16)
    G(f)
    GD_CTXA = g_ctxa.val

    def f(v):
        v.wait_ge(g_ctxa.sem, GD_CTXA)
        v.wait_ge(gp.sem, GP_CTXD)
        for j in range(8):
            dv.inc(v.tensor_copy(ctxd[0:64, j, 0:64], ctxa[0:64, j, :]))
            dv.inc(v.tensor_copy(ctxd[64:128, j, 64:128], ctxa[64:128, j, :]))
    V(f)
    DV_CTXD = dv.val
    GD_CTXD = GD_CTXA

    p1.close()
    pw = ExitStack()
    w1_sb = pw.enter_context(nc.sbuf_tensor("w1_sb", [128, 8, DFF], BF16))
    p2 = ExitStack()
    SB2 = lambda n, s, d: p2.enter_context(nc.sbuf_tensor(n, s, d))
    wq_sb = SB2("wq_sb", [128, 8, D], BF16)
    wo_sb = SB2("wo_sb", [128, 8, D], BF16)
    qsb = SB2("qsb", [128, 8, TT], BF16)
    atn = SB2("atn", [128, 8, TT], BF16)
    aof = SB2("aof", [128, 8, TT], BF16)
    xfm2 = [SB2(f"xfm2_{i}", [128, 8, 1024], BF16) for i in range(2)]
    atm = [SB2(f"atm{i}", [128, D], BF16) for i in range(2)]
    xtm = [SB2(f"xtm{i}", [128, D], F32) for i in range(2)]
    xtb = [SB2(f"xtb{i}", [128, D], BF16) for i in range(2)]
    amat = SB2("amat", [128, NG, 128], BF16)
    ahi = SB2("ahi", [128, NG, 128], BF16)
    alo = SB2("alo", [128, NG, 128], BF16)
    bmat = SB2("bmat", [128, NG, 160], BF16)

    def f(g):
        g.wait_ge(cc1.sem, 1)
        for dd in range(8):
            g_wqo.inc(g.dma_start(out=wq_sb[:, dd, :], in_=wq_in[128 * dd:128 * (dd + 1), :]), 16)
            g_wqo.inc(g.dma_start(out=wo_sb[:, dd, :], in_=wo_in[128 * dd:128 * (dd + 1), :]), 16)
        for dd in range(8):
            for q_ in range(2):
                g_w1.inc(g.dma_start(out=w1_sb[:, dd, 2048 * q_:2048 * (q_ + 1)],
                                     in_=w1_in[128 * dd:128 * (dd + 1), 2048 * q_:2048 * (q_ + 1)]), 16)
    G(f)
    GD_W1 = g_w1.val
    GD_WQO = g_wqo.val

    def f(t):
        t.wait_ge(g_wr.sem, GD_WR)
        bw(t, 6)
        for dd in range(8):
            mi = t.matmul(bank[6][0:1, 0:8], ones128[:, :], wr_sb[:, dd, :],
                          start=(dd == 0), stop=(dd == 7))
        mm.inc(mi)
    T(f)
    WM = mm.val

    def f(v, WM=WM):
        v.wait_ge(mm.sem, WM)
        dv.inc(v.tensor_copy(csr[:, :], bank[6][0:1, 0:8]))
    V(f)
    bank_free[6] = (dv, dv.val)
    DV_CSR = dv.val

    def f(t):
        t.wait_ge(dv.sem, DV_CSR)
        bw(t, 7)
        mm.inc(t.matmul(bank[7][:, 0:8], onesn1[0:1, :], csr[0:1, :], start=True, stop=True))
    T(f)
    WM = mm.val

    def f(v, WM=WM):
        v.wait_ge(mm.sem, WM)
        dv.inc(v.tensor_copy(csn[:, :], bank[7][:, 0:8]))
    V(f)
    bank_free[7] = (dv, dv.val)

    MT2 = TOK // 1024
    mm_u_done = {}
    aoT_cd = {}
    for tt in range(MT2):
        bi = tt % 2

        def f(s, tt=tt, bi=bi):
            if tt == 0:
                s.wait_ge(sd.sem, SD_CONST)
                s.wait_ge(g_xo.sem, GD_XO)
                s.wait_ge(cc1.sem, 1)
            for dd in range(8):
                xsem[bi].inc(s.dma_start(
                    out=xfm2[bi][:, dd, :],
                    in_=xbfo[1024 * tt:1024 * (tt + 1), 128 * dd:128 * (dd + 1)],
                    transpose=True), 16)
        SY(f)
        xfm_wm[(2, tt)] = xsem[bi].val

        for sub in range(2):
            t4 = 2 * tt + sub
            sl = slice(512 * sub, 512 * (sub + 1))

            dvb = dv.val
            q_wm = []

            def f(t, tt=tt, bi=bi, sub=sub, dvb=dvb):
                if tt == 0 and sub == 0:
                    t.wait_ge(g_wqo.sem, GD_WQO)
                if sub == 0:
                    t.wait_ge(xsem[bi].sem, xfm_wm[(2, tt)])
                for m in range(8):
                    bk = m % 2
                    bw(t, bk)
                    for dd in range(8):
                        mi = t.matmul(bank[bk][:, :], wq_sb[:, dd, 128 * m:128 * (m + 1)],
                                      xfm2[bi][:, dd, 512 * sub:512 * (sub + 1)],
                                      start=(dd == 0), stop=(dd == 7))
                    mm.inc(mi)
                    q_wm.append(mm.val)
                    bank_free[bk] = (dv, dvb + m + 1)
            T(f)

            def f(v, q_wm=q_wm):
                for m in range(8):
                    v.wait_ge(mm.sem, q_wm[m])
                    dv.inc(v.tensor_copy(qsb[:, m, :], bank[m % 2][:, :]))
            V(f)
            DV_Q = dv.val

            acb = ac.val
            at_wm = []

            def f(t, tt=tt, sub=sub, DV_Q=DV_Q, acb=acb):
                t.wait_ge(dv.sem, DV_Q)
                if tt == 0 and sub == 0:
                    t.wait_ge(dv.sem, DV_CTXD)
                for j in range(8):
                    bk = 2 + j % 2
                    bw(t, bk)
                    mm.inc(t.matmul(bank[bk][:, :], ctxd[:, j, :], qsb[:, j, :],
                                    start=True, stop=True))
                    at_wm.append(mm.val)
                    bank_free[bk] = (ac, acb + j + 1)
            T(f)

            def f(sc, at_wm=at_wm):
                for j in range(8):
                    sc.wait_ge(mm.sem, at_wm[j])
                    ac.inc(sc.copy(atn[:, j, :], bank[2 + j % 2][:, :]))
            SC(f)
            AC_AT = ac.val

            acb2 = ac.val
            ao_wm = []

            def f(t, AC_AT=AC_AT, acb2=acb2):
                t.wait_ge(ac.sem, AC_AT)
                for o in range(8):
                    bk = 4 + o % 2
                    bw(t, bk)
                    for ad in range(8):
                        mi = t.matmul(bank[bk][:, :], wo_sb[:, ad, 128 * o:128 * (o + 1)],
                                      atn[:, ad, :], start=(ad == 0), stop=(ad == 7))
                    mm.inc(mi)
                    ao_wm.append(mm.val)
                    bank_free[bk] = (ac, acb2 + o + 1)
            T(f)

            cdb_ao = cd.val

            def f(sc, t4=t4, ao_wm=ao_wm, cdb_ao=cdb_ao):
                for o in range(8):
                    sc.wait_ge(mm.sem, ao_wm[o])
                    if t4 >= 1 and o == 0:
                        sc.wait_ge(cd.sem, cdb_ao)
                    ac.inc(sc.activation(aof[:, o, :], bank[4 + o % 2][:, :], AF.Identity,
                                         bias=bout_sb[:, o:o + 1]))
                    sc.wait_ge(ac.sem, ac.val)
                    cd.inc(sc.dma_start(out=aoT[128 * o:128 * (o + 1), TT * t4:TT * (t4 + 1)],
                                        in_=aof[:, o, :]), 16)
            SC(f)
            aoT_cd[t4] = cd.val
            AC_AO = ac.val

            def f(t, tt=tt, bi=bi, sub=sub, AC_AO=AC_AO):
                t.wait_ge(ac.sem, AC_AO)
                bw(t, 6)
                for dd in range(8):
                    t.matmul(bank[6][0:8, :], wr_sb[:, dd, :],
                             xfm2[bi][:, dd, 512 * sub:512 * (sub + 1)],
                             start=(dd == 0), stop=False)
                for dd in range(8):
                    mi = t.matmul(bank[6][0:8, :], wr_sb[:, dd, :], aof[:, dd, :],
                                  start=False, stop=(dd == 7))
                mm.inc(mi)
            T(f)
            mm_u_done[tt] = mm.val
            WM = mm.val

            def f(v, WM=WM):
                v.wait_ge(mm.sem, WM)
                dv.inc(v.tensor_copy(u_T[:, :], bank[6][0:8, :]))
            V(f)
            bank_free[6] = (dv, dv.val)
            DV_UT = dv.val

            def f(t, DV_UT=DV_UT):
                t.wait_ge(dv.sem, DV_UT)
                bw(t, 7)
                for q4 in range(4):
                    mi = t.transpose(bank[7][:, 8 * q4:8 * (q4 + 1)],
                                     u_T[:, 128 * q4:128 * (q4 + 1)], id8[:, :])
                    mm.inc(mi)
            T(f)
            WM = mm.val

            def f(v, t4=t4, WM=WM):
                v.wait_ge(mm.sem, WM)
                for q4 in range(4):
                    dv.inc(v.tensor_copy(lg[:, 4 * t4 + q4, :], bank[7][:, 8 * q4:8 * (q4 + 1)]))
            V(f)
            bank_free[7] = (dv, dv.val)

    DV_LG = dv.val
    CD_AOT = cd.val

    ztile = xtm
    dv_z, cd_zw = {}, {}
    for g_ in range(NG):
        par = g_ % 2

        def f(s, g_=g_, par=par):
            s.wait_ge(cd.sem, aoT_cd[g_ // 4])
            if g_ >= 2:
                s.wait_ge(dv.sem, dv_z[g_ - 2])
            sa[par].inc(s.dma_start(out=atm[par][:, :], in_=aoT[0:D, 128 * g_:128 * (g_ + 1)],
                                    transpose=True), 16)
        SY(f)
        SD_ATM = sa[par].val

        def f(sc, g_=g_, par=par):
            if g_ == 0:
                sc.wait_ge(g_xo.sem, GD_XO)
            if g_ >= 2:
                sc.wait_ge(dv.sem, dv_z[g_ - 2])
            csem[par].inc(sc.dma_start(out=xtb[par][:, :],
                                       in_=xbfo[128 * g_:128 * (g_ + 1), :]), 16)
        SC(f)
        C_X = csem[par].val

        def f(v, g_=g_, par=par, SD_ATM=SD_ATM, C_X=C_X):
            v.wait_ge(sa[par].sem, SD_ATM)
            v.wait_ge(csem[par].sem, C_X)
            if g_ >= 2:
                v.wait_ge(gz[par].sem, cd_zw[g_ - 2])
            dv.inc(v.scalar_tensor_tensor(ztile[par][:, :], atm[par][:, :], 1.0, xtb[par][:, :],
                                          A.mult, A.add, accum_out=zsum[:, g_:g_ + 1]))
            v.wait_ge(dv.sem, dv.val)
            dv.inc(v.scalar_tensor_tensor(zsc[:, :], ztile[par][:, :], 1.0, ztile[par][:, :],
                                          A.mult, A.mult, accum_out=zsq[:, g_:g_ + 1]))
            dv_z[g_] = dv.val
        V(f)

        def f(g, g_=g_, par=par):
            g.wait_ge(dv.sem, dv_z[g_])
            gz[par].inc(g.dma_start(out=zscr[128 * g_:128 * (g_ + 1), :], in_=ztile[par][:, :]), 16)
            cd_zw[g_] = gz[par].val
        G(f)

    def f(v):
        v.wait_ge(dv.sem, dv_z[NG - 1])
        dv.inc(v.tensor_scalar(meanv[:, :], zsum[:, :], 1.0 / D, None, A.mult))
        v.wait_ge(dv.sem, dv.val)
        dv.inc(v.tensor_tensor(tmpg[:, :], meanv[:, :], meanv[:, :], A.mult))
        v.wait_ge(dv.sem, dv.val)
        dv.inc(v.scalar_tensor_tensor(tmpg2[:, :], zsq[:, :], 1.0 / D, tmpg[:, :],
                                      A.mult, A.subtract))
    V(f)
    WM = dv.val

    def f(sc, WM=WM):
        sc.wait_ge(dv.sem, WM)
        ac.inc(sc.activation(tmpg[:, :], tmpg2[:, :], AF.Sqrt, bias=epss[:, 0:1]))
    SC(f)
    WM = ac.val

    def f(v, WM=WM):
        v.wait_ge(ac.sem, WM)
        dv.inc(v.reciprocal(rstd[:, :], tmpg[:, :]))
    V(f)
    DV_RSTD1 = dv.val

    dv_y, cd_y, gd_y = {}, {}, {}
    for g_ in range(NG):
        par = g_ % 2

        def f(g, g_=g_, par=par):
            if g_ == 0:
                g.wait_ge(gz[0].sem, gz[0].val)
                g.wait_ge(gz[1].sem, gz[1].val)
            if g_ >= 2:
                g.wait_ge(dv.sem, dv_y[g_ - 2])
                g.wait_ge(gy[par].sem, gd_y[g_ - 2])
            gsem[par].inc(g.dma_start(out=xtm[par][:, :],
                                      in_=zscr[128 * g_:128 * (g_ + 1), :]), 16)
        G(f)
        C_Z = gsem[par].val

        def f(v, g_=g_, par=par, C_Z=C_Z):
            v.wait_ge(gsem[par].sem, C_Z)
            v.wait_ge(dv.sem, DV_RSTD1)
            if g_ >= 2:
                v.wait_ge(gy[par].sem, gd_y[g_ - 2])
            dv.inc(v.tensor_scalar(ztile[par][:, :], xtm[par][:, :], meanv[:, g_:g_ + 1],
                                   rstd[:, g_:g_ + 1], A.subtract, A.mult))
            dv_y[g_] = dv.val
        V(f)

        def f(g, g_=g_, par=par):
            g.wait_ge(dv.sem, dv_y[g_])
            gy[par].inc(g.dma_start(out=yag_in[128 * g_:128 * (g_ + 1), :], in_=ztile[par][:, :]), 16)
            if stage == "y":
                gy[par].inc(g.dma_start(out=ydbg[128 * g_:128 * (g_ + 1), :], in_=ztile[par][:, :]), 16)
            gd_y[g_] = gy[par].val
        G(f)


    bNG8 = (128, NG, 8)
    r3 = lambda x: x.rearrange("p (g o) -> p g o", o=1)

    def f(v):
        v.wait_ge(dv.sem, DV_LG)
        dv.inc(v.tensor_tensor(lga[:, :, :],
                               csn[:, :].rearrange("p (o e) -> p o e", o=1).broadcast_to(bNG8),
                               r3(meanv[:, :]).broadcast_to(bNG8), A.mult))
        v.wait_ge(dv.sem, dv.val)
        dv.inc(v.tensor_tensor(lga[:, :, :], lga[:, :, :], lg[:, :, :], A.add))
        v.wait_ge(dv.sem, dv.val)
        dv.inc(v.reduce_max(lmax[:, :], lga[:, :, :], axis=AX.X))
        v.wait_ge(dv.sem, dv.val)
        dv.inc(v.tensor_tensor(lgs[:, :, :], lga[:, :, :],
                               r3(lmax[:, :]).broadcast_to(bNG8), A.subtract))
        v.wait_ge(dv.sem, dv.val)
        dv.inc(v.tensor_tensor(lgs[:, :, :], lgs[:, :, :],
                               r3(rstd[:, :]).broadcast_to(bNG8), A.mult))
    V(f)
    WM = dv.val

    def f(sc, WM=WM):
        sc.wait_ge(dv.sem, WM)
        ac.inc(sc.activation(lgs[:, :, :], lgs[:, :, :], AF.Exp))
    SC(f)
    AC_REXP = ac.val

    def f(v):
        v.wait_ge(ac.sem, AC_REXP)
        dv.inc(v.reduce_sum(se[:, :], lgs[:, :, :], axis=AX.X))
        v.wait_ge(dv.sem, dv.val)
        dv.inc(v.reciprocal(pmax[:, :], se[:, :]))
        for g_ in range(NG):
            dv.inc(v.max_index(aidx[:, :], lmax[:, g_:g_ + 1].broadcast_to((128, 8)),
                               lga[:, g_, :]))
            v.wait_ge(dv.sem, dv.val)
            dv.inc(v.tensor_copy(etok[:, g_:g_ + 1], aidx[:, 0:1]))
            v.wait_ge(dv.sem, dv.val)
        dv.inc(v.tensor_tensor(ohbf_all[:, :, :],
                               iota8[:, :].rearrange("p (o e) -> p o e", o=1).broadcast_to(bNG8),
                               r3(etok[:, :]).broadcast_to(bNG8), A.is_equal))
    V(f)
    DV_OH = dv.val

    def f(t):
        t.wait_ge(dv.sem, DV_OH)
        bw(t, 6)
        mm.inc(t.matmul(bank[6][0:1, 0:128], ones128[:, :],
                        ohbf_all[:, :, :].rearrange("p g e -> p (g e)"), start=True, stop=True))
    T(f)
    WM = mm.val

    def f(v, WM=WM):
        v.wait_ge(mm.sem, WM)
        dv.inc(v.tensor_copy(crow[:, :], bank[6][0:1, 0:128]))
    V(f)
    bank_free[6] = (dv, dv.val)
    DV_CROW = dv.val

    def f(sc):
        sc.wait_ge(dv.sem, DV_CROW)
        cd.inc(sc.dma_start(out=crow_h.ap(), in_=crow[0:1, :]), 16)
        sc.wait_ge(cd.sem, cd.val)
        cd.inc(sc.dma_start(out=cnts[:, :],
                            in_=crow_h.ap().rearrange("o (g e) -> (o g) e", g=16)), 16)
    SC(f)
    CD_CNTS = cd.val

    def f(t):
        t.wait_ge(cd.sem, CD_CNTS)
        bw(t, 7)
        mm.inc(t.matmul(bank[7][0:1, 0:8], ones16[:, :], cnts[:, :], start=True, stop=True))
    T(f)
    WM = mm.val

    def f(v, WM=WM):
        v.wait_ge(mm.sem, WM)
        dv.inc(v.tensor_copy(cntrow[:, :], bank[7][0:1, 0:8]))
    V(f)
    bank_free[7] = (dv, dv.val)
    DV_CNTROW = dv.val

    def f(g):
        gp.inc(g.memset(zsc[0:16, :], 0.0))
        g.wait_ge(gp.sem, gp.val)
        g_crow.inc(g.dma_start(out=yag_in[2048:2064, :], in_=zsc[0:16, :]), 16)
        g.wait_ge(g_crow.sem, g_crow.val)
        g.wait_ge(dv.sem, DV_CNTROW)
        g_crow.inc(g.dma_start(out=yag_in[2048:2049, 0:16], in_=cntrow[:, :].bitcast(BF16)), 16)
    G(f)
    GD_CROW = g_crow.val

    def f(g):
        g.wait_ge(gy[0].sem, gy[0].val)
        g.wait_ge(gy[1].sem, gy[1].val)
        g.wait_ge(g_crow.sem, GD_CROW)
        cc2.inc(g.collective_compute("AllGather", A.bypass, replica_groups=RG,
                                     ins=[yag_in.ap()], outs=[yag_out.ap()]), 1)
    G(f)

    def f(g):
        g.wait_ge(cc2.sem, 1)
        g_call.inc(g.dma_start(
            out=call[:, :],
            in_=yag_out.ap().rearrange("(r t) d -> r t d", t=2064)[:, 2048, 0:16].bitcast(F32)), 16)
    G(f)
    GD_CALL = g_call.val

    def f(t):
        t.wait_ge(g_call.sem, GD_CALL)
        bw(t, 6)
        mm.inc(t.matmul(bank[6][0:1, 0:8], pref[:, :], call[:, :], start=True, stop=True))
    T(f)
    WM = mm.val

    def f(v, WM=WM):
        v.wait_ge(mm.sem, WM)
        dv.inc(v.tensor_copy(offs[:, :], bank[6][0:1, 0:8]))
    V(f)
    bank_free[6] = (dv, dv.val)
    DV_OFFS = dv.val

    def f(t):
        t.wait_ge(dv.sem, DV_OFFS)
        t.wait_ge(dv.sem, DV_CROW)
        bw(t, 7)
        mm.inc(t.transpose(bank[7][0:128, 0:1], crow[0:1, :], id1[:, :]))
        mm.inc(t.transpose(bank[7][0:8, 1:2], offs[0:1, :], id1[:, :]))
    T(f)
    WM = mm.val

    def f(v, WM=WM):
        v.wait_ge(mm.sem, WM)
        dv.inc(v.tensor_copy(ccol[:, :], bank[7][0:128, 0:1]))
        dv.inc(v.tensor_copy(ocol[:, :], bank[7][0:8, 1:2]))
    V(f)
    bank_free[7] = (dv, dv.val)
    WM = dv.val

    def f(t, WM=WM):
        t.wait_ge(dv.sem, WM)
        bw(t, 6)
        t.matmul(bank[6][0:128, 0:1], LI128[:, :], ccol[:, :], start=True, stop=False)
        mm.inc(t.matmul(bank[6][0:128, 0:1], OJ8[:, :], ocol[:, :], start=False, stop=True))
    T(f)
    WM = mm.val

    def f(v, WM=WM):
        v.wait_ge(mm.sem, WM)
        dv.inc(v.tensor_copy(carry_col[:, :], bank[6][0:128, 0:1]))
    V(f)
    bank_free[6] = (dv, dv.val)
    WM = dv.val

    def f(t, WM=WM):
        t.wait_ge(dv.sem, WM)
        bw(t, 7)
        mm.inc(t.transpose(bank[7][0:1, 0:128], carry_col[:, :], id128[:, :]))
    T(f)
    WM = mm.val

    def f(v, WM=WM):
        v.wait_ge(mm.sem, WM)
        dv.inc(v.tensor_copy(carry_row[:, :], bank[7][0:1, 0:128]))
    V(f)
    bank_free[7] = (dv, dv.val)
    WM = dv.val

    def f(t, WM=WM):
        t.wait_ge(dv.sem, WM)
        t.wait_ge(dv.sem, DV_OH)
        bw(t, 6)
        t.matmul(bank[6][:, 0:128], trilL[:, :],
                 ohbf_all[:, :, :].rearrange("p g e -> p (g e)"), start=True, stop=False)
        mm.inc(t.matmul(bank[6][:, 0:128], ones1[0:1, :], carry_row[0:1, :],
                        start=False, stop=True))
    T(f)
    WM = mm.val

    def f(v, WM=WM):
        v.wait_ge(mm.sem, WM)
        dv.inc(v.tensor_tensor(posc[:, :, :],
                               bank[6][:, 0:128].rearrange("p (g e) -> p g e", g=NG),
                               ohbf_all[:, :, :], A.mult))
        v.wait_ge(dv.sem, dv.val)
        dv.inc(v.reduce_sum(post[:, :], posc[:, :, :], axis=AX.X))
    V(f)
    bank_free[6] = (dv, dv.val)
    DV_POS = dv.val

    bNG128 = (128, NG, 128)
    bNG160 = (128, NG, 160)

    def f(v):
        v.wait_ge(dv.sem, DV_POS)
        dv.inc(v.tensor_scalar(kept[:, :], post[:, :], 2560.0, None, A.is_lt))
        v.wait_ge(dv.sem, dv.val)
        dv.inc(v.tensor_tensor(kf[:, :], kept[:, :], pmax[:, :], A.mult))
        dv.inc(v.tensor_scalar(k2[:, :], kept[:, :], -1.0, 2.0, A.mult, A.add))
        dv.inc(v.tensor_scalar(tmpg[:, :], post[:, :], 2559.0, None, A.min))
        v.wait_ge(dv.sem, dv.val)
        dv.inc(v.tensor_copy(iscr[:, 3 * NG:4 * NG], tmpg[:, :]))
        v.wait_ge(dv.sem, dv.val)
        dv.inc(v.tensor_scalar(iscr[:, 4 * NG:5 * NG], iscr[:, 3 * NG:4 * NG], 511, None,
                               A.bitwise_and))
        dv.inc(v.tensor_scalar(iscr[:, 3 * NG:4 * NG], iscr[:, 3 * NG:4 * NG], 9, None,
                               A.logical_shift_right))
        v.wait_ge(dv.sem, dv.val)
        dv.inc(v.tensor_copy(tmpg[:, :], iscr[:, 4 * NG:5 * NG]))
        dv.inc(v.tensor_copy(tmpg2[:, :], iscr[:, 3 * NG:4 * NG]))
        v.wait_ge(dv.sem, dv.val)
        dv.inc(v.scalar_tensor_tensor(tmpg[:, :], etok[:, :], 512.0, tmpg[:, :],
                                      A.mult, A.add))
        v.wait_ge(dv.sem, dv.val)
        dv.inc(v.scalar_tensor_tensor(tmpg2[:, :], tmpg2[:, :], 4096.0, tmpg[:, :],
                                      A.mult, A.add))
        v.wait_ge(dv.sem, dv.val)
        dv.inc(v.tensor_copy(growi[:, :], tmpg2[:, :]))
        dv.inc(v.tensor_copy(iscr[:, 0:NG], post[:, :]))
        v.wait_ge(dv.sem, dv.val)
        dv.inc(v.tensor_scalar(iscr[:, NG:2 * NG], iscr[:, 0:NG], 7, None,
                               A.logical_shift_right))
        dv.inc(v.tensor_scalar(iscr[:, 2 * NG:3 * NG], iscr[:, 0:NG], 127, None,
                               A.bitwise_and))
        v.wait_ge(dv.sem, dv.val)
        dv.inc(v.tensor_copy(tmpg[:, :], iscr[:, NG:2 * NG]))
        dv.inc(v.tensor_copy(tmpg2[:, :], iscr[:, 2 * NG:3 * NG]))
        v.wait_ge(dv.sem, dv.val)
        dv.inc(v.scalar_tensor_tensor(tmpg[:, :], etok[:, :], 20.0, tmpg[:, :],
                                      A.mult, A.add))
        v.wait_ge(dv.sem, dv.val)
        dv.inc(v.tensor_tensor(amat[:, :, :],
                               iota128[:, :].rearrange("p (o k) -> p o k", o=1).broadcast_to(bNG128),
                               r3(tmpg2[:, :]).broadcast_to(bNG128), A.is_equal))
        dv.inc(v.tensor_tensor(bmat[:, :, :],
                               iota160[:, :].rearrange("p (o k) -> p o k", o=1).broadcast_to(bNG160),
                               r3(tmpg[:, :]).broadcast_to(bNG160), A.is_equal))
        v.wait_ge(dv.sem, dv.val)
        dv.inc(v.tensor_tensor(tmpg[:, :], kept[:, :], tokhi[:, :], A.mult))
        dv.inc(v.tensor_tensor(tmpg2[:, :], kept[:, :], toklo[:, :], A.mult))
        v.wait_ge(dv.sem, dv.val)
        dv.inc(v.tensor_tensor(ahi[:, :, :], amat[:, :, :],
                               r3(tmpg[:, :]).broadcast_to(bNG128), A.mult))
        dv.inc(v.tensor_tensor(alo[:, :, :], amat[:, :, :],
                               r3(tmpg2[:, :]).broadcast_to(bNG128), A.mult))
    V(f)
    DV_AB = dv.val

    def f(t):
        t.wait_ge(dv.sem, DV_AB)
        bw(t, 4)
        bw(t, 5)
        for g_ in range(NG):
            mi = t.matmul(bank[4][:, 0:160], ahi[:, g_, :], bmat[:, g_, :],
                          start=(g_ == 0), stop=(g_ == NG - 1))
        mm.inc(mi)
        for g_ in range(NG):
            mi = t.matmul(bank[5][:, 0:160], alo[:, g_, :], bmat[:, g_, :],
                          start=(g_ == 0), stop=(g_ == NG - 1))
        mm.inc(mi)
    T(f)
    MM_S2 = mm.val

    def f(v):
        v.wait_ge(mm.sem, MM_S2)
        dv.inc(v.tensor_copy(s2sb[:, :], bank[4][:, 0:160]))
        v.wait_ge(dv.sem, dv.val)
        dv.inc(v.tensor_tensor(s2sb[:, :], s2sb[:, :], bank[5][:, 0:160], A.add))
    V(f)
    bank_free[4] = (dv, dv.val)
    bank_free[5] = (dv, dv.val)
    DV_S2 = dv.val

    def f(sc):
        sc.wait_ge(dv.sem, DV_S2)
        cd.inc(sc.dma_start(out=s2_in.ap(), in_=s2sb[:, :]), 16)
    SC(f)
    CD_S2 = cd.val

    def f(g):
        g.wait_ge(cd.sem, CD_S2)
        cc4.inc(g.collective_compute("AllReduce", A.add, replica_groups=RG,
                                     ins=[s2_in.ap()], outs=[s2_out.ap()]), 1)
        g.wait_ge(cc4.sem, 1)
        g_s2l.inc(g.dma_start(out=s2sb[:, :], in_=s2_out.ap()), 16)
    G(f)
    GD_S2 = g_s2l.val

    def f(v):
        v.wait_ge(g_s2l.sem, GD_S2)
        dv.inc(v.tensor_copy(s2i[:, :], s2sb[:, :]))
        v.wait_ge(dv.sem, dv.val)
        dv.inc(v.tensor_copy(s2my[:, :], s2i[:, ds(dynv["cb_v"], 20)]))
    V(f)
    DV_S2I = dv.val

    p2.close()
    p3s = ExitStack()
    SB3 = lambda n, s, d: p3s.enter_context(nc.sbuf_tensor(n, s, d))
    w2_sb = SB3("w2_sb", [128, 32, D], BF16)
    rows = [SB3(f"rows{i}", [128, 8, TT], BF16) for i in range(2)]
    h_sb = SB3("h_sb", [128, 32, TT], BF16)
    dtile = [SB3(f"dtile{i}", [128, D], BF16) for i in range(2)]
    eotile = [SB3("eotile0", [128, TT], BF16)]
    eotile = [eotile[0], eotile[0]]
    gtile = dtile
    z2 = SB3("z2", [128, D], F32)
    z2all = w1_sb[:, 0:4, :].rearrange("p a (g q) -> p (a g) q", g=4)
    z2b = w1_sb[:, 4:5, :].rearrange("p a b -> p (a b)").bitcast(F32)[:, 0:D]
    ycbb = [SB3("ycbb0", [128, D], BF16),
            w1_sb[:, 5:6, 0:D].rearrange("p a b -> p (a b)"),
            w1_sb[:, 5:6, D:2 * D].rearrange("p a b -> p (a b)"),
            w1_sb[:, 5:6, 2 * D:3 * D].rearrange("p a b -> p (a b)")]
    gtile4 = [w1_sb[:, 6:7, 0:D].rearrange("p a b -> p (a b)"),
              w1_sb[:, 6:7, D:2 * D].rearrange("p a b -> p (a b)"),
              w1_sb[:, 6:7, 2 * D:3 * D].rearrange("p a b -> p (a b)"),
              w1_sb[:, 6:7, 3 * D:4 * D].rearrange("p a b -> p (a b)")]

    def f(g):
        g.wait_ge(dv.sem, DV_S2I)
        for ff in range(32):
            g_w2.inc(g.dma_start(out=w2_sb[:, ff, :], in_=w2_in[128 * ff:128 * (ff + 1), :]), 16)
    G(f)
    GD_W2 = g_w2.val

    sd_disp = {}
    for j in range(CAP // 128):
        def f(g, j=j):
            if j == 0:
                g.wait_ge(dv.sem, DV_S2I)
                g.wait_ge(cc2.sem, 1)
            if j >= 2:
                g.wait_ge(sa[j % 2].sem, sd_disp[j - 2])
            gsem[j % 2].inc(g.indirect_dma_start(
                dtile[j % 2][:, :], None, yag_out.ap(),
                IOA(s2my[:, j:j + 1], 0)), 16)
        G(f)
        WM = gsem[j % 2].val

        def f(s, j=j, WM=WM):
            s.wait_ge(gsem[j % 2].sem, WM)
            sa[j % 2].inc(s.dma_start(out=dispT[128 * j:128 * (j + 1), :], in_=dtile[j % 2][:, :]), 16)
            sd_disp[j] = sa[j % 2].val
        SY(f)
    SD_DISP = sd.val

    NRT = CAP // TT
    mm_row_done = {}
    dtile_last = [None, None]
    for r in range(NRT):
        bi = r % 2

        def f(s, r=r, bi=bi):
            s.wait_ge(sa[0].sem, sd_disp[4 * r + 2])
            s.wait_ge(sa[1].sem, sd_disp[4 * r + 3])
            if r >= 2:
                s.wait_ge(mm.sem, mm_row_done[r - 2])
            for dd in range(8):
                xsem[bi].inc(s.dma_start(
                    out=rows[bi][:, dd, :],
                    in_=dispT[TT * r:TT * (r + 1), 128 * dd:128 * (dd + 1)],
                    transpose=True), 16)
        SY(f)
        xfm_wm[(3, r)] = xsem[bi].val

        acb = ac.val
        h_wm = []

        def f(t, r=r, bi=bi, acb=acb):
            if r == 0:
                t.wait_ge(g_w1.sem, GD_W1)
            t.wait_ge(xsem[bi].sem, xfm_wm[(3, r)])
            for m_ in range(32):
                bk = m_ % 2
                bw(t, bk)
                for dd in range(8):
                    mi = t.matmul(bank[bk][:, :], w1_sb[:, dd, 128 * m_:128 * (m_ + 1)],
                                  rows[bi][:, dd, :], start=(dd == 0), stop=(dd == 7))
                mm.inc(mi)
                h_wm.append(mm.val)
                bank_free[bk] = (ac, acb + m_ + 1)
        T(f)
        MM_HROW = mm.val

        def f(sc, r=r, h_wm=h_wm):
            if r >= 1:
                sc.wait_ge(mm.sem, mm_row_done[r - 1])
            for m_ in range(32):
                sc.wait_ge(mm.sem, h_wm[m_])
                ac.inc(sc.activation(h_sb[:, m_, :], bank[m_ % 2][:, :], AF.Relu))
        SC(f)
        AC_H = ac.val

        dvb = dv.val
        cdb = cd.val
        eo_wm = []

        def f(t, r=r, AC_H=AC_H, dvb=dvb):
            t.wait_ge(ac.sem, AC_H)
            if r == 0:
                t.wait_ge(g_w2.sem, GD_W2)
            for o in range(8):
                bk = 2 + o % 2
                bw(t, bk)
                for ff in range(32):
                    mi = t.matmul(bank[bk][:, :], w2_sb[:, ff, 128 * o:128 * (o + 1)],
                                  h_sb[:, ff, :], start=(ff == 0), stop=(ff == 31))
                mm.inc(mi)
                eo_wm.append(mm.val)
                bank_free[bk] = (dv, dvb + o + 1)
        T(f)
        mm_row_done[r] = mm.val

        def f(v, eo_wm=eo_wm, cdb=cdb):
            for o in range(8):
                v.wait_ge(mm.sem, eo_wm[o])
                if o >= 1:
                    v.wait_ge(cd.sem, cdb + 16 * o)
                dv.inc(v.tensor_copy(eotile[o % 2][:, :], bank[2 + o % 2][:, :]))
        V(f)
        DV_EO = dv.val

        def f(sc, r=r, dvb=dvb):
            for o in range(8):
                sc.wait_ge(dv.sem, dvb + o + 1)
                cd.inc(sc.dma_start(out=eoT[128 * o:128 * (o + 1), TT * r:TT * (r + 1)],
                                    in_=eotile[o % 2][:, :]), 16)
        SC(f)
        CD_EOR = cd.val

        def f(s, r=r, CD_EOR=CD_EOR):
            s.wait_ge(cd.sem, CD_EOR)
            if r == 0:
                s.wait_ge(sa[0].sem, sd_disp[CAP // 128 - 2])
                s.wait_ge(sa[1].sem, sd_disp[CAP // 128 - 1])
            for j in range(4 * r, 4 * r + 4):
                if dtile_last[j % 2] is not None:
                    sm_, v_ = dtile_last[j % 2]
                    s.wait_ge(sm_.sem, v_)
                sme[j % 2].inc(s.dma_start(out=dtile[j % 2][:, :],
                                          in_=eoT[0:D, 128 * j:128 * (j + 1)], transpose=True), 16)
                s.wait_ge(sme[j % 2].sem, sme[j % 2].val)
                smg[j % 2].inc(s.dma_start(out=ag3_in[128 * j:128 * (j + 1), :],
                                           in_=dtile[j % 2][:, :]), 16)
                dtile_last[j % 2] = (smg[j % 2], smg[j % 2].val)
        SY(f)
        AG3W = (smg[0].val, smg[1].val)

        def f(g, r=r, AG3W=AG3W):
            g.wait_ge(smg[0].sem, AG3W[0])
            g.wait_ge(smg[1].sem, AG3W[1])
            cc5.inc(g.collective_compute("AllGather", A.bypass, replica_groups=RG,
                                         ins=[ag3_in.ap()[512 * r:512 * (r + 1), :]],
                                         outs=[ag3_out.ap()[4096 * r:4096 * (r + 1), :]]), 1)
        G(f)
    CD_EOT = cd.val

    z2p = [z2, z2b]
    dv_z2, dv_out, cd_out = {}, {}, {}
    for g_ in range(NG):
        par = g_ % 2
        q4 = g_ % 4

        def f(g, g_=g_, par=par, q4=q4):
            if g_ == 0:
                g.wait_ge(cc5.sem, 5)
            if g_ >= 4:
                g.wait_ge(dv.sem, dv_z2[g_ - 4])
            gq[q4].inc(g.indirect_dma_start(
                gtile4[q4][:, :], None, ag3_out.ap(),
                IOA(growi[:, g_:g_ + 1], 0)), 16)
        G(f)
        G_GAT = gq[q4].val

        def f(sc, g_=g_, par=par, q4=q4):
            if g_ >= 4:
                sc.wait_ge(dv.sem, dv_z2[g_ - 4])
            cq[q4].inc(sc.dma_start(out=ycbb[q4][:, :],
                                     in_=yag_in[128 * g_:128 * (g_ + 1), :]), 16)
        SC(f)
        C_Y = cq[q4].val

        def f(v, g_=g_, par=par, q4=q4, G_GAT=G_GAT, C_Y=C_Y):
            v.wait_ge(gq[q4].sem, G_GAT)
            v.wait_ge(cq[q4].sem, C_Y)
            if g_ >= 2:
                v.wait_ge(dv.sem, dv_z2[g_ - 2])
            dv.inc(v.tensor_scalar(z2p[par][:, :], gtile4[q4][:, :], kf[:, g_:g_ + 1], None, A.mult))
            v.wait_ge(dv.sem, dv.val)
            dv.inc(v.scalar_tensor_tensor(z2p[par][:, :], ycbb[q4][:, :], k2[:, g_:g_ + 1],
                                          z2p[par][:, :], A.mult, A.add,
                                          accum_out=zsum[:, g_:g_ + 1]))
            v.wait_ge(dv.sem, dv.val)
            dv.inc(v.tensor_copy(z2all[:, g_:g_ + 1, :].rearrange("p a b -> p (a b)"), z2p[par][:, :]))
            v.wait_ge(dv.sem, dv.val)
            dv.inc(v.scalar_tensor_tensor(zsc[:, :], z2p[par][:, :], 1.0, z2p[par][:, :],
                                          A.mult, A.mult, accum_out=zsq[:, g_:g_ + 1]))
            dv_z2[g_] = dv.val
        V(f)

    def f(v):
        v.wait_ge(dv.sem, dv_z2[NG - 1])
        dv.inc(v.tensor_scalar(meanv[:, :], zsum[:, :], 1.0 / D, None, A.mult))
        v.wait_ge(dv.sem, dv.val)
        dv.inc(v.tensor_tensor(tmpg[:, :], meanv[:, :], meanv[:, :], A.mult))
        v.wait_ge(dv.sem, dv.val)
        dv.inc(v.scalar_tensor_tensor(tmpg2[:, :], zsq[:, :], 1.0 / D, tmpg[:, :],
                                      A.mult, A.subtract))
    V(f)
    WM = dv.val

    def f(sc, WM=WM):
        sc.wait_ge(dv.sem, WM)
        ac.inc(sc.activation(tmpg[:, :], tmpg2[:, :], AF.Sqrt, bias=epss[:, 0:1]))
    SC(f)
    WM = ac.val

    def f(v, WM=WM):
        v.wait_ge(ac.sem, WM)
        dv.inc(v.reciprocal(rstd[:, :], tmpg[:, :]))
    V(f)
    DV_RSTD2 = dv.val

    zpar = [z2, z2b]
    for g_ in range(NG):
        par = g_ % 2

        def f(v, g_=g_, par=par):
            v.wait_ge(dv.sem, DV_RSTD2)
            if g_ >= 2:
                v.wait_ge(cdo[par].sem, cd_out[g_ - 2])
            dv.inc(v.tensor_scalar(zpar[par][:, :], z2all[:, g_:g_ + 1, :].rearrange("p a b -> p (a b)"), meanv[:, g_:g_ + 1],
                                   rstd[:, g_:g_ + 1], A.subtract, A.mult))
            dv_out[g_] = dv.val
        V(f)

        def f(sc, g_=g_, par=par):
            sc.wait_ge(dv.sem, dv_out[g_])
            cdo[par].inc(sc.dma_start(out=out_ext[128 * g_:128 * (g_ + 1), :], in_=zpar[par][:, :]), 16)
            cd_out[g_] = cdo[par].val
        SC(f)

    def f(sc):
        sc.wait_ge(cdo[0].sem, cdo[0].val)
        sc.wait_ge(cdo[1].sem, cdo[1].val)
        sc.wait_ge(cd.sem, cd.val)
        sc.wait_ge(cd2.sem, cd2.val)
        sc.wait_ge(cd3.sem, cd3.val)
    SC(f)

    def f(s):
        s.wait_ge(sd.sem, sd.val)
        s.wait_ge(sd2.sem, sd2.val)
        s.wait_ge(sd3.sem, sd3.val)
    SY(f)

    def f(g):
        for s_ in [gd, g2, g3, g_wr, g_xo, g_wkv, g_wqo, g_w1, g_w2, g_ctxa,
                   g_call, g_s2l, g_crow, g_z2, gq[0], gq[1], gq[2], gq[3],
                   gy[0], gy[1], gz[0], gz[1],
                   sa[0], sa[1], sme[0], sme[1], smg[0], smg[1]] + gx:
            g.wait_ge(s_.sem, s_.val)
    G(f)

    p3s.close()
    pw.close()
    ctx.close()
    return nc


def make_inmaps(x, W_qkv, W_out, b_out, Wr, W1, W2):
    x = np.asarray(x)
    in_maps = []
    for c in range(NCORES):
        b, h = c // 2, c % 2
        tok0 = np.arange(128, dtype=np.float32)[:, None] + 128.0 * np.arange(NG, dtype=np.float32)[None, :]
        tokid = tok0 + 2064.0 * c
        in_maps.append({
            "xb": np.ascontiguousarray(x.reshape(4, NB, D)[b]),
            "wk": np.ascontiguousarray(W_qkv[:, D + 512 * h: D + 512 * (h + 1)]),
            "wv": np.ascontiguousarray(W_qkv[:, 2 * D + 512 * h: 2 * D + 512 * (h + 1)]),
            "wq": np.ascontiguousarray(W_qkv[:, 0:D]),
            "wo": np.ascontiguousarray(W_out),
            "bout2": np.ascontiguousarray(np.asarray(b_out).reshape(8, 128).T),
            "wr": np.ascontiguousarray(Wr),
            "w1e": np.ascontiguousarray(W1[c]),
            "w2e": np.ascontiguousarray(W2[c]),
            "tokhi": np.floor(tokid / 128.0) * 128.0,
            "toklo": tokid - np.floor(tokid / 128.0) * 128.0,
            "prefmask": (np.arange(E) < c).astype(np.float32).reshape(E, 1),
            "dyn": np.array([[20 * c, 512 * 2 * b, 2048 * h, 0]], dtype=np.int32),
        })
    return in_maps


_NC_CACHE = {}


def _get_nc():
    if "nc" not in _NC_CACHE:
        from concourse.library_overlay import lower_extended_insts
        nc = build_nc(stage="full")
        lower_extended_insts(nc)
        _NC_CACHE["nc"] = nc
    return _NC_CACHE["nc"]


def kernel(L0_attn_x, L0_attn_W_qkv, L0_attn_W_out, L0_attn_b_out,
           L0_ffn_Wr, L0_ffn_W1, L0_ffn_W2):
    from concourse.bass_utils import run_bass_kernel_spmd
    x = np.ascontiguousarray(np.asarray(L0_attn_x), dtype=np.float32)
    nc = _get_nc()
    in_maps = make_inmaps(x.reshape(-1, D), np.asarray(L0_attn_W_qkv),
                          np.asarray(L0_attn_W_out), np.asarray(L0_attn_b_out),
                          np.asarray(L0_ffn_Wr), np.asarray(L0_ffn_W1),
                          np.asarray(L0_ffn_W2))
    res = run_bass_kernel_spmd(nc, in_maps, core_ids=list(range(NCORES)))
    out = np.concatenate([r["out"] for r in res.results], axis=0)
    return out.reshape(4, NB, D)
